# version 01
# baseline (speedup 1.0000x reference)
import numpy as np
import ml_dtypes
import concourse.bass as bass
import concourse.mybir as mybir
from concourse.bass_utils import run_bass_kernel_spmd
from concourse.alu_op_type import AluOpType as OP

F32 = mybir.dt.float32
BF16 = mybir.dt.bfloat16
I16 = mybir.dt.int16
AF = mybir.ActivationFunctionType
AX = mybir.AxisListType.X

S, D, T, H, HD, F = 2048, 1024, 16, 16, 64, 4096
EPS = 1e-5
NEGBIG = -1.0e38

_UPTO = 99
_B3H = 16
_B3NORM = True
_B3LVL = 4
_B6T = 16
_B6TEVOFF = False
_DUMP = None


def _attn_pieces():
    pieces = []
    for j in range(T):
        qc = j * 128
        first = True
        while qc < S:
            w = min(512 - (qc % 512), S - qc)
            pieces.append(dict(j=j, qc=qc, w=w, diag=first,
                               stop=(first and j % 4 == 3)))
            qc += w
            first = False
    return pieces


def build_nc():
    nc = bass.Bass("TRN2", target_bir_lowering=False,
                   detect_race_conditions=False)

    x_d = nc.dram_tensor("x", [S, D], F32, kind="ExternalInput").ap()
    wat_d = nc.dram_tensor("w_attn", [D, 3 * D], F32, kind="ExternalInput").ap()
    bat_d = nc.dram_tensor("b_attn", [3 * D], F32, kind="ExternalInput").ap()
    wpr_d = nc.dram_tensor("w_proj", [D, D], F32, kind="ExternalInput").ap()
    bpr_d = nc.dram_tensor("b_proj", [D], F32, kind="ExternalInput").ap()
    w1t_d = nc.dram_tensor("w1t", [D, F], F32, kind="ExternalInput").ap()
    b1_d = nc.dram_tensor("b1", [F], F32, kind="ExternalInput").ap()
    w2t_d = nc.dram_tensor("w2t", [F, D], F32, kind="ExternalInput").ap()
    b2_d = nc.dram_tensor("b2", [D], F32, kind="ExternalInput").ap()
    idm_d = nc.dram_tensor("idm", [128, 128], BF16, kind="ExternalInput").ap()
    cm_d = nc.dram_tensor("cmask", [128, 128], F32, kind="ExternalInput").ap()
    on_d = nc.dram_tensor("onesr", [1, 128], BF16, kind="ExternalInput").ap()
    out_d = nc.dram_tensor("out", [S, D], F32, kind="ExternalOutput").ap()

    _BASE = (nc.sbuf_base + 31) & ~31

    def at(name, shape, dtype, offset):
        return nc.alloc_sbuf_tensor_at(name, shape, dtype, offset=offset + _BASE)
    xres = at("xres", [128, T * D], BF16, offset=0)
    qkT = at("qkT", [128, 16 * S], BF16, offset=32768)
    ln1T = at("ln1T", [128, 8 * S], BF16, offset=98304)
    vpad = at("vpad", [128, T * 1040], BF16, offset=131072)
    R4 = 164352
    rowf = at("rowf", [1, 4096], F32, offset=R4)
    rowb = at("rowb", [1, 4096], BF16, offset=180736)
    wqk = at("wqk", [128, 2 * 4096], BF16, offset=R4)
    wv = at("wv", [128, 8 * 1024], BF16, offset=R4)
    wproj = at("wproj", [128, 8 * 1024], BF16, offset=R4)
    wstr = at("wstr", [128, 4096], F32, offset=R4)
    rrow = at("rrow", [1, 2048], F32, offset=R4)
    rrowb = at("rrowb", [1, 2048], BF16, offset=R4 + 8192)
    rb = at("rb", [128, 512], BF16, offset=R4 + 12288)
    hhalf = at("hhalf", [128, 2048], F32, offset=R4)
    g2b = at("g2b", [128, 4096], BF16, offset=R4 + 8192)
    R5 = 180736
    probs = at("probs", [128, 2 * 2048], BF16, offset=R5)
    tln = at("tln", [128, D], F32, offset=R5)
    uln = at("uln", [128, D], F32, offset=R5 + 4096)
    wqs = at("wqs", [128, 2048], F32, offset=R5)
    zq = at("zq", [128, 1024], BF16, offset=R5)
    thq = at("thq", [128, 1024], BF16, offset=R5 + 2048)
    otile = at("otile", [128, D], F32, offset=R5)
    dq = at("dq", [128, 1024], BF16, offset=R5 + 4096)
    uq = at("uq", [128, 1024], BF16, offset=R5 + 6144)
    ofc = at("ofc", [128, D], F32, offset=R5 + 4096)
    b1b = at("b1b", [128, F], BF16, offset=188928)
    b2b = at("b2b", [128, D], BF16, offset=197120)
    bvb = at("bvb", [128, D], BF16, offset=199168)
    bprb = at("bprb", [128, D], BF16, offset=201216)
    ident = at("ident", [128, 128], BF16, offset=203264)
    cmask = at("cmask_s", [128, 128], F32, offset=203520)
    onesr = at("onesr_s", [1, 128], BF16, offset=204032)
    onesc = at("onesc", [128, 1], F32, offset=204288)
    batfm = at("batfm", [128, 16], F32, offset=204320)
    stt_ = at("stats", [128, 400], F32, offset=204384)
    shtile = at("shtile", [128, 1024], BF16, offset=205984)
    tiq = at("tiq", [128, 1024], I16, offset=208032)
    mrowf = at("mrowf", [1, 16], F32, offset=210080)
    onesf = at("onesf", [1, 128], F32, offset=210272)
    mrow = at("mrow", [1, 16], BF16, offset=210144)
    epsc = at("epsc", [128, 1], F32, offset=210208)
    negonec = at("negonec", [128, 1], F32, offset=210240)

    ps_acc = nc.alloc_psum_tensor("ps_acc", [128, 2048], F32)
    ps_a = nc.alloc_psum_tensor("ps_a", [128, 512], F32)
    ps_b = nc.alloc_psum_tensor("ps_b", [128, 512], F32)
    ps_c = nc.alloc_psum_tensor("ps_c", [128, 512], F32)
    ps_t = nc.alloc_psum_tensor("ps_t", [128, 1024], BF16)

    SB = dict(xres=xres, qkT=qkT, ln1T=ln1T, vpad=vpad, stt_=stt_, g2b=g2b)

    def sems(*names):
        return [nc.alloc_semaphore(n) for n in names]

    def n2(v):
        v.drain()

    def ln_merge(v, nt, bco, mco, varco, tA, tB, tC):
        st = stt_[:, bco:bco + nt * 12].rearrange("p (t c) -> p t c", c=12)
        mA, mB, mC, mD_ = st[:, :, 1], st[:, :, 4], st[:, :, 7], st[:, :, 10]
        vA, vB, vC, vD = st[:, :, 2], st[:, :, 5], st[:, :, 8], st[:, :, 11]
        M = stt_[:, mco:mco + nt]
        VAR = stt_[:, varco:varco + nt]
        TA = stt_[:, tA:tA + nt]
        TB = stt_[:, tB:tB + nt]
        TC = stt_[:, tC:tC + nt]
        v.drain()
        v.tensor_tensor(TA, mA, mB, OP.add)
        v.tensor_tensor(TB, mC, mD_, OP.add)
        v.tensor_tensor(VAR, vA, vB, OP.add)
        v.tensor_tensor(TC, vC, vD, OP.add)
        n2(v)
        v.tensor_tensor(M, TA, TB, OP.add)
        v.tensor_tensor(VAR, VAR, TC, OP.add)
        n2(v)
        v.tensor_scalar(M, M, 0.25, None, OP.mult)
        n2(v)
        v.scalar_tensor_tensor(TA, M, -1.0, mA, OP.mult, OP.add)
        n2(v)
        v.tensor_tensor(TA, TA, TA, OP.mult)
        n2(v)
        last = None
        for mm_ in (mB, mC, mD_):
            v.scalar_tensor_tensor(TB, M, -1.0, mm_, OP.mult, OP.add)
            n2(v)
            v.tensor_tensor(TB, TB, TB, OP.mult)
            n2(v)
            v.tensor_tensor(TA, TA, TB, OP.add)
            n2(v)
        v.tensor_scalar(TA, TA, 0.25, None, OP.mult)
        n2(v)
        last = v.scalar_tensor_tensor(VAR, VAR, 1.0 / 1024.0, TA, OP.mult, OP.add)
        return last

    def rstd(a, nt, varco, rco):
        a.activation(stt_[:, rco:rco + nt], stt_[:, varco:varco + nt], AF.Ln,
                     bias=epsc[:])
        n2(a)
        return a.activation(stt_[:, rco:rco + nt], stt_[:, rco:rco + nt], AF.Exp,
                            scale=-0.5)

    (s0d, s0g) = sems("s0d", "s0g")
    with nc.Block("b0a") as blk:
        @blk.sync
        def _(sync):
            sync.dma_start(ident[:], idm_d[:]).then_inc(s0d, 16)
            sync.dma_start(cmask[:], cm_d[:]).then_inc(s0d, 16)
            sync.dma_start(onesr[:], on_d[:]).then_inc(s0d, 16)

        @blk.gpsimd
        def _(g):
            xg = x_d.rearrange("(g t p) d -> g p t d", p=128, t=4)
            for gi in range(4):
                g.dma_start(xres[:, gi * 4096:(gi + 1) * 4096], xg[gi]).then_inc(s0g, 16)
            for c in range(16):
                g.dma_start(batfm[:, c:c + 1],
                            bat_d[c * 128:(c + 1) * 128].rearrange("(a p) -> p a", p=128)
                            ).then_inc(s0g, 16)

        @blk.vector
        def _(v):
            v.memset(onesc[:], 1.0)
            v.memset(epsc[:], EPS)
            v.memset(onesf[:], 1.0)
            v.memset(negonec[:], -1.0)
            for j in range(T):
                ap = vpad[:, j * 1040:j * 1040 + 1040] \
                    .rearrange("p (h c) -> p h c", c=65)[:, :, 64:65]
                v.memset(ap, 1.0)
            v.wait_ge(s0g, 16 * 20)
            v.wait_ge(s0d, 16 * 3)

    for bname, src, width, dstb in [("b1", b1_d[:], F, b1b),
                                    ("b2", b2_d[:], D, b2b),
                                    ("bv", bat_d[2048:3072], D, bvb),
                                    ("bp", bpr_d[:], D, bprb)]:
        (sd, sv, sp) = sems(f"s0{bname}d", f"s0{bname}v", f"s0{bname}p")
        npc = (width + 511) // 512
        with nc.Block("b0" + bname) as blk:
            @blk.sync
            def _(sync, src=src, width=width, sd=sd):
                sync.dma_start(rowf[:1, 0:width], src).then_inc(sd, 16)

            @blk.vector
            def _(v, width=width, dstb=dstb, sd=sd, sv=sv, sp=sp, npc=npc):
                v.wait_ge(sd, 16)
                v.tensor_copy(rowb[:1, 0:width], rowf[:1, 0:width]).then_inc(sv, 1)
                for p in range(npc):
                    w = min(512, width - p * 512)
                    v.wait_ge(sp, p + 1)
                    v.tensor_copy(dstb[:, p * 512:p * 512 + w], ps_c[:, 0:w]
                                  ).then_inc(sv, 1)

            @blk.tensor
            def _(t, width=width, sv=sv, sp=sp, npc=npc):
                t.wait_ge(sv, 1)
                for p in range(npc):
                    w = min(512, width - p * 512)
                    if p > 0:
                        t.wait_ge(sv, 1 + p)
                    t.matmul(ps_c[0:128, 0:w], onesr[:1, :],
                             rowb[:1, p * 512:p * 512 + w],
                             start=True, stop=True).then_inc(sp, 1)

    if _UPTO >= 1:
        _b1(nc, sems, locals())
    if _UPTO >= 2:
        _b2(nc, sems, locals())
    if _UPTO >= 3:
        _b3(nc, sems, locals())
    if _UPTO >= 4:
        _b4(nc, sems, locals())
    if _UPTO >= 5:
        _b5(nc, sems, locals())
    if _UPTO >= 6:
        _b6(nc, sems, locals())
    if _DUMP is not None:
        _dump(nc, sems, locals())
    return nc, SB


C_BN, C_M, C_R, C_NB, C_V, C_TA, C_TB, C_TC = 0, 192, 208, 224, 240, 256, 272, 288
W_SC = 376


def _b1(nc, sems, L):
    g = lambda n: L[n]
    stt_, xres, shtile, ln1T, ident, ps_t = (
        g("stt_"), g("xres"), g("shtile"), g("ln1T"), g("ident"), g("ps_t"))
    rstd, ln_merge, n2 = g("rstd"), g("ln_merge"), g("n2")
    (s1v, s1a, s1p) = sems("s1v", "s1a", "s1p")
    with nc.Block("b1") as blk:
        @blk.vector
        def _(v):
            for i in range(T):
                v.bn_stats(stt_[:, C_BN + i * 12:C_BN + i * 12 + 6],
                           xres[:, i * D:i * D + 512])
                v.bn_stats(stt_[:, C_BN + i * 12 + 6:C_BN + i * 12 + 12],
                           xres[:, i * D + 512:i * D + 1024])
            ln_merge(v, T, C_BN, C_M, C_V, C_TA, C_TB, C_TC).then_inc(s1v, 1)
            v.wait_ge(s1a, 1)
            v.scalar_tensor_tensor(stt_[:, C_NB:C_NB + T], stt_[:, C_M:C_M + T],
                                   -1.0, stt_[:, C_R:C_R + T], OP.mult, OP.mult
                                   ).then_inc(s1v, 1)
            for i in range(T):
                v.wait_ge(s1p, i + 1)
                cp = None
                for c in range(8):
                    cp = v.tensor_copy(ln1T[:, c * S + i * 128:c * S + (i + 1) * 128],
                                       ps_t[:, c * 128:(c + 1) * 128])
                cp.then_inc(s1v, 1)

        @blk.scalar
        def _(a):
            a.wait_ge(s1v, 1)
            rstd(a, T, C_V, C_R).then_inc(s1a, 1)
            a.wait_ge(s1v, 2)
            for i in range(T):
                if i > 0:
                    a.wait_ge(s1p, i)
                a.activation(shtile[:], xres[:, i * D:(i + 1) * D], AF.Identity,
                             scale=stt_[:, C_R + i:C_R + i + 1],
                             bias=stt_[:, C_NB + i:C_NB + i + 1]).then_inc(s1a, 1)

        @blk.tensor
        def _(t):
            for i in range(T):
                t.wait_ge(s1a, i + 2)
                if i > 0:
                    t.wait_ge(s1v, 2 + i)
                mm = None
                for c in range(8):
                    mm = t.transpose(ps_t[:, c * 128:(c + 1) * 128],
                                     shtile[:, c * 128:(c + 1) * 128], ident[:])
                mm.then_inc(s1p, 1)


def _b2(nc, sems, L):
    g = lambda n: L[n]
    (wat_d, wqk, wv, ln1T, qkT, vpad, batfm, bvb, ps_acc, ps_a, ps_b) = (
        g("wat_d"), g("wqk"), g("wv"), g("ln1T"), g("qkT"), g("vpad"),
        g("batfm"), g("bvb"), g("ps_acc"), g("ps_a"), g("ps_b"))
    (s2g, s2p, s2a, s2v) = sems("s2g", "s2p", "s2a", "s2v")
    with nc.Block("b2") as blk:
        @blk.gpsimd
        def _(gp):
            for gi in range(4):
                if gi >= 2:
                    gp.wait_ge(s2p, 4 * (gi - 1))
                for d in range(8):
                    gp.dma_start(
                        wqk[:, (gi % 2) * 4096 + d * 512:(gi % 2) * 4096 + (d + 1) * 512],
                        wat_d[d * 128:(d + 1) * 128, gi * 512:(gi + 1) * 512]
                    ).then_inc(s2g, 16)
            gp.wait_ge(s2p, 16)
            for d in range(8):
                gp.dma_start(wv[:, d * 1024:(d + 1) * 1024],
                             wat_d[d * 128:(d + 1) * 128, 2048:3072]).then_inc(s2g, 16)

        @blk.tensor
        def _(t):
            for gi in range(4):
                t.wait_ge(s2g, 16 * 8 * (gi + 1))
                for sp in range(4):
                    idx = gi * 4 + sp
                    if idx > 0:
                        t.wait_ge(s2a, idx)
                    mm = None
                    for fi in range(4):
                        for d in range(8):
                            mm = t.matmul(
                                ps_acc[:, fi * 512:(fi + 1) * 512],
                                wqk[:, (gi % 2) * 4096 + d * 512 + fi * 128:
                                    (gi % 2) * 4096 + d * 512 + (fi + 1) * 128],
                                ln1T[:, d * S + sp * 512:d * S + (sp + 1) * 512],
                                start=(d == 0), stop=(d == 7))
                    mm.then_inc(s2p, 1)
            t.wait_ge(s2g, 16 * 40)
            for i in range(T):
                for fh in range(2):
                    k = 2 * i + fh
                    if k >= 2:
                        t.wait_ge(s2v, k - 1)
                    bank = ps_a if fh == 0 else ps_b
                    mm = None
                    for d in range(8):
                        mm = t.matmul(bank[:],
                                      ln1T[:, d * S + i * 128:d * S + (i + 1) * 128],
                                      wv[:, d * 1024 + fh * 512:d * 1024 + (fh + 1) * 512],
                                      start=(d == 0), stop=(d == 7))
                    mm.then_inc(s2p, 1)

        @blk.scalar
        def _(a):
            for gi in range(4):
                for sp in range(4):
                    idx = gi * 4 + sp
                    a.wait_ge(s2p, idx + 1)
                    cp = None
                    for fi in range(4):
                        c = gi * 4 + fi
                        cp = a.activation(
                            qkT[:, c * S + sp * 512:c * S + (sp + 1) * 512],
                            ps_acc[:, fi * 512:(fi + 1) * 512], AF.Identity,
                            bias=batfm[:, c:c + 1])
                    cp.then_inc(s2a, 1)

        @blk.vector
        def _(v):
            for i in range(T):
                for fh in range(2):
                    k = 2 * i + fh
                    v.wait_ge(s2p, 16 + k + 1)
                    bank = ps_a if fh == 0 else ps_b
                    dst = vpad[:, i * 1040 + fh * 520:i * 1040 + (fh + 1) * 520] \
                        .rearrange("p (h c) -> p h c", c=65)[:, :, 0:64]
                    v.tensor_tensor(dst,
                                    bank[:].rearrange("p (h c) -> p h c", c=64),
                                    bvb[:, fh * 512:(fh + 1) * 512]
                                    .rearrange("p (h c) -> p h c", c=64),
                                    OP.add).then_inc(s2v, 1)


def _b3(nc, sems, L):
    g = lambda n: L[n]
    (qkT, vpad, probs, cmask, rrow, rrowb, rb, onesr, ps_acc, ps_a, ps_b, ps_c) = (
        g("qkT"), g("vpad"), g("probs"), g("cmask"), g("rrow"), g("rrowb"),
        g("rb"), g("onesr"), g("ps_acc"), g("ps_a"), g("ps_b"), g("ps_c"))
    n2 = g("n2")
    pieces = _attn_pieces()
    NP = len(pieces)
    (s3p, s3m, s3e, s3q, s3v, s3c, s3n, s3r) = sems(
        "s3p", "s3m", "s3e", "s3q", "s3v", "s3c", "s3n", "s3r")
    cumj = []
    cc = 0
    for j in range(T):
        cc += len([p for p in pieces if p["j"] == j])
        cumj.append(cc)
    diag_ids = [k for k, p in enumerate(pieces) if p["diag"]]
    nd = len(diag_ids)
    mask_of = {k: i for i, k in enumerate(diag_ids)}

    def kslice(h, j):
        c = 8 + h // 2
        r0 = (h % 2) * 64
        return qkT[r0:r0 + 64, c * S + j * 128:c * S + (j + 1) * 128]

    def qslice(h, qc, w):
        c = h // 2
        r0 = (h % 2) * 64
        return qkT[r0:r0 + 64, c * S + qc:c * S + qc + w]

    by_j = {}
    for k, p in enumerate(pieces):
        by_j.setdefault(p["j"], []).append((k, p))

    with nc.Block("b3") as blk:
        @blk.tensor
        def _(t):
            for h in range(_B3H):
                def emit_pv(jj):
                    for k, p in by_j[jj]:
                        gid = h * NP + k
                        t.wait_ge(s3e, gid + 1)
                        if p["j"] == 0 and h > 0:
                            t.wait_ge(s3v, h * 6)
                        if _B3LVL < 4:
                            t.nop().then_inc(s3q, 1)
                            continue
                        po = (p["j"] % 2) * 2048 + (p["qc"] - p["j"] * 128)
                        t.matmul(ps_acc[0:65, p["qc"]:p["qc"] + p["w"]],
                                 vpad[:, p["j"] * 1040 + h * 65:
                                      p["j"] * 1040 + h * 65 + 65],
                                 probs[:, po:po + p["w"]],
                                 start=(p["j"] == 0), stop=p["stop"]
                                 ).then_inc(s3q, 1)

                for jj in range(T):
                    for k, p in by_j[jj]:
                        gid = h * NP + k
                        if gid >= 2:
                            t.wait_ge(s3e, gid - 1)
                        bank = ps_a if gid % 2 == 0 else ps_b
                        if _B3LVL < 1:
                            t.nop().then_inc(s3p, 1)
                        else:
                            t.matmul(bank[:, 0:p["w"]], kslice(h, p["j"]),
                                     qslice(h, p["qc"], p["w"]),
                                     start=True, stop=True).then_inc(s3p, 1)
                    if jj > 0:
                        emit_pv(jj - 1)
                emit_pv(T - 1)
                if not _B3NORM:
                    continue
                t.wait_ge(s3v, h * 6 + 2)
                for b in range(4):
                    if b > 0:
                        t.wait_ge(s3c, h * 4 + b)
                    t.matmul(ps_c[0:64, 0:512], onesr[:1, 0:64],
                             rrowb[:1, b * 512:(b + 1) * 512],
                             start=True, stop=True).then_inc(s3n, 1)

        @blk.vector
        def _(v):
            for h in range(_B3H):
                for jj in range(T):
                    k = diag_ids[jj]
                    gid = h * NP + k
                    v.wait_ge(s3p, gid + 1)
                    bank = ps_a if gid % 2 == 0 else ps_b
                    if _B3LVL < 2:
                        v.nop().then_inc(s3m, 1)
                    else:
                        v.tensor_tensor(bank[:, 0:128], bank[:, 0:128], cmask[:],
                                        OP.add).then_inc(s3m, 1)
                    if jj >= 2:
                        v.wait_ge(s3q, h * NP + cumj[jj - 2])
                        v.nop().then_inc(s3r, 1)
                v.wait_ge(s3q, (h + 1) * NP)
                if not _B3NORM:
                    for _ in range(6):
                        v.nop().then_inc(s3v, 1)
                    continue
                v.reciprocal(rrow[:1, :], ps_acc[64:65, 0:2048]).then_inc(s3v, 1)
                n2(v)
                v.tensor_copy(rrowb[:1, :], rrow[:1, :]).then_inc(s3v, 1)
                for b in range(4):
                    v.wait_ge(s3c, h * 4 + b + 1)
                    r0 = (h % 2) * 64
                    dst = qkT[r0:r0 + 64,
                              (h // 2) * S + b * 512:(h // 2) * S + (b + 1) * 512]
                    v.tensor_tensor(dst, ps_acc[0:64, b * 512:(b + 1) * 512],
                                    rb[0:64, :], OP.mult).then_inc(s3v, 1)

        @blk.scalar
        def _(a):
            for h in range(_B3H):
                for k, p in enumerate(pieces):
                    gid = h * NP + k
                    if p["diag"]:
                        a.wait_ge(s3m, h * nd + mask_of[k] + 1)
                    else:
                        a.wait_ge(s3p, gid + 1)
                    if p["j"] >= 2:
                        a.wait_ge(s3r, h * (T - 2) + p["j"] - 1)
                    bank = ps_a if gid % 2 == 0 else ps_b
                    po = (p["j"] % 2) * 2048 + (p["qc"] - p["j"] * 128)
                    if _B3LVL < 3:
                        a.nop().then_inc(s3e, 1)
                    else:
                        a.activation(probs[:, po:po + p["w"]],
                                     bank[:, 0:p["w"]], AF.Exp).then_inc(s3e, 1)
                if not _B3NORM:
                    continue
                for b in range(4):
                    a.wait_ge(s3n, h * 4 + b + 1)
                    a.wait_ge(s3v, h * 6 + 2 + b)
                    a.copy(rb[0:64, :], ps_c[0:64, 0:512]).then_inc(s3c, 1)


def _b4(nc, sems, L):
    g = lambda n: L[n]
    (wpr_d, wproj, qkT, bprb, tln, uln, xres, stt_, ps_a, ps_b) = (
        g("wpr_d"), g("wproj"), g("qkT"), g("bprb"), g("tln"), g("uln"),
        g("xres"), g("stt_"), g("ps_a"), g("ps_b"))
    rstd, ln_merge, n2 = g("rstd"), g("ln_merge"), g("n2")
    (s4g, s4p, s4v, s4a) = sems("s4g", "s4p", "s4v", "s4a")
    BN, M_, V_, R_, NB_, TA_, TB_, TC_ = 0, 12, 13, 14, 15, 16, 17, 18
    with nc.Block("b4") as blk:
        @blk.gpsimd
        def _(gp):
            for d in range(8):
                gp.dma_start(wproj[:, d * 1024:(d + 1) * 1024],
                             wpr_d[d * 128:(d + 1) * 128, :]).then_inc(s4g, 16)

        @blk.tensor
        def _(t):
            t.wait_ge(s4g, 16 * 8)
            for i in range(T):
                for eh in range(2):
                    if i > 0:
                        t.wait_ge(s4v, 5 * (i - 1) + eh + 1)
                    bank = ps_a if eh == 0 else ps_b
                    mm = None
                    for d in range(8):
                        mm = t.matmul(bank[:],
                                      qkT[:, d * S + i * 128:d * S + (i + 1) * 128],
                                      wproj[:, d * 1024 + eh * 512:d * 1024 + (eh + 1) * 512],
                                      start=(d == 0), stop=(d == 7))
                    mm.then_inc(s4p, 1)

        @blk.vector
        def _(v):
            for i in range(T):
                for eh in range(2):
                    v.wait_ge(s4p, 2 * i + eh + 1)
                    bank = ps_a if eh == 0 else ps_b
                    v.tensor_tensor(tln[:, eh * 512:(eh + 1) * 512], bank[:],
                                    bprb[:, eh * 512:(eh + 1) * 512],
                                    OP.add).then_inc(s4v, 1)
                if i > 0:
                    v.wait_ge(s4a, 2 * i)
                v.drain()
                v.bn_stats(stt_[:, BN:BN + 6], tln[:, 0:512])
                v.bn_stats(stt_[:, BN + 6:BN + 12], tln[:, 512:1024])
                ln_merge(v, 1, BN, M_, V_, TA_, TB_, TC_).then_inc(s4v, 1)
                v.wait_ge(s4a, 2 * i + 1)
                v.scalar_tensor_tensor(stt_[:, NB_:NB_ + 1], stt_[:, M_:M_ + 1],
                                       -1.0, stt_[:, R_:R_ + 1], OP.mult, OP.mult
                                       ).then_inc(s4v, 1)
                v.wait_ge(s4a, 2 * i + 2)
                v.tensor_tensor(xres[:, i * D:(i + 1) * D], xres[:, i * D:(i + 1) * D],
                                uln[:], OP.add).then_inc(s4v, 1)

        @blk.scalar
        def _(a):
            for i in range(T):
                a.wait_ge(s4v, 5 * i + 3)
                rstd(a, 1, V_, R_).then_inc(s4a, 1)
                a.wait_ge(s4v, 5 * i + 4)
                a.activation(uln[:], tln[:], AF.Identity,
                             scale=stt_[:, R_:R_ + 1],
                             bias=stt_[:, NB_:NB_ + 1]).then_inc(s4a, 1)


def _b5(nc, sems, L):
    g = lambda n: L[n]
    (w1t_d, w2t_d, wstr, wqs, qkT, ln1T, vpad, stt_, onesc, onesr,
     mrowf, mrow, ps_c, onesf) = (
        g("w1t_d"), g("w2t_d"), g("wstr"), g("wqs"), g("qkT"), g("ln1T"),
        g("vpad"), g("stt_"), g("onesc"), g("onesr"), g("mrowf"), g("mrow"),
        g("ps_c"), g("onesf"))
    n2 = g("n2")
    (s5d, s5v, s5a, s5p, s5r) = sems("s5d", "s5v", "s5a", "s5p", "s5r")
    W1S, W2S = 0, 32
    SUM1, SUM2 = W_SC + 6, W_SC + 7
    MW1, MW2 = W_SC, W_SC + 1
    THR1, THR2 = W_SC + 2, W_SC + 3
    NTH1, NTH2 = W_SC + 4, W_SC + 5
    w1p = [(d, q) for d in range(8) for q in range(4)]

    def w1src(k):
        d, q = w1p[k]
        return w1t_d[d * 128:(d + 1) * 128, q * 1024:(q + 1) * 1024]

    def w2src(k):
        return w2t_d[k * 128:(k + 1) * 128, :]

    with nc.Block("b5") as blk:
        @blk.sync
        def _(sync):
            for k in range(64):
                if k >= 4:
                    sync.wait_ge(s5a, k - 3)
                sync.dma_start(wstr[:, (k % 4) * 1024:(k % 4) * 1024 + 1024],
                               w1src(k) if k < 32 else w2src(k - 32)
                               ).then_inc(s5d, 16)
            sync.wait_ge(s5v, 6)
            for k in range(64):
                if k >= 4:
                    sync.wait_ge(s5v, 6 + 2 * (k - 3))
                sync.dma_start(wstr[:, (k % 4) * 1024:(k % 4) * 1024 + 1024],
                               w1src(k) if k < 32 else w2src(k - 32)
                               ).then_inc(s5d, 16)

        @blk.scalar
        def _(a):
            for k in range(64):
                a.wait_ge(s5d, 16 * (k + 1))
                if k >= 2:
                    a.wait_ge(s5r, k - 1)
                a.activation(wqs[:, (k % 2) * 1024:(k % 2) * 1024 + 1024],
                             wstr[:, (k % 4) * 1024:(k % 4) * 1024 + 1024],
                             AF.Abs).then_inc(s5a, 1)

        @blk.vector
        def _(v):
            for k in range(64):
                v.wait_ge(s5a, k + 1)
                col = W1S + k if k < 32 else W2S + (k - 32)
                v.tensor_reduce(stt_[:, col:col + 1],
                                wqs[:, (k % 2) * 1024:(k % 2) * 1024 + 1024],
                                AX, OP.add).then_inc(s5r, 1)
            v.drain()
            v.tensor_reduce(stt_[:, SUM1:SUM1 + 1], stt_[:, W1S:W1S + 32], AX,
                            OP.add).then_inc(s5v, 1)
            v.drain()
            v.tensor_reduce(stt_[:, SUM2:SUM2 + 1], stt_[:, W2S:W2S + 32], AX,
                            OP.add).then_inc(s5v, 1)
            v.wait_ge(s5p, 2)
            v.tensor_scalar(mrowf[:1, 0:2], ps_c[0:1, 0:2], 1.0 / (D * F), 1e-5,
                            OP.mult, OP.max).then_inc(s5v, 1)
            v.drain()
            v.nop().then_inc(s5v, 1)
            v.wait_ge(s5p, 3)
            v.tensor_copy(stt_[:, MW1:MW1 + 2], ps_c[:, 64:66])
            v.drain()
            v.tensor_scalar(stt_[:, THR1:THR1 + 2], stt_[:, MW1:MW1 + 2],
                            0.5, None, OP.mult).then_inc(s5v, 1)
            v.drain()
            v.tensor_scalar(stt_[:, NTH1:NTH1 + 2], stt_[:, THR1:THR1 + 2],
                            -1.0, None, OP.mult).then_inc(s5v, 1)
            for k in range(64):
                v.wait_ge(s5d, 16 * (64 + k + 1))
                v.drain()
                src_ = wstr[:, (k % 4) * 1024:(k % 4) * 1024 + 1024]
                cmp_ = wqs[:, (k % 2) * 1024:(k % 2) * 1024 + 1024]
                if k < 32:
                    d, q = w1p[k]
                    dst = qkT[:, d * 4096 + q * 1024:d * 4096 + (q + 1) * 1024]
                    thr_, nth_ = THR1, NTH1
                else:
                    kk = k - 32
                    dst = (ln1T[:, kk * 1024:(kk + 1) * 1024] if kk < 16
                           else vpad[:, (kk - 16) * 1024:(kk - 15) * 1024])
                    thr_, nth_ = THR2, NTH2
                v.tensor_single_scalar(cmp_, src_, stt_[:, thr_:thr_ + 1],
                                       OP.is_gt).then_inc(s5v, 1)
                v.drain()
                v.scalar_tensor_tensor(dst, src_, stt_[:, nth_:nth_ + 1],
                                       cmp_, OP.is_lt, OP.subtract
                                       ).then_inc(s5v, 1)

        @blk.tensor
        def _(t):
            for wi in range(2):
                t.wait_ge(s5v, wi + 1)
                t.matmul(ps_c[0:1, wi:wi + 1], stt_[:, SUM1 + wi:SUM1 + wi + 1],
                         onesc[:], start=True, stop=True).then_inc(s5p, 1)
            t.wait_ge(s5v, 4)
            t.matmul(ps_c[0:128, 64:66], onesf[:1, :], mrowf[:1, 0:2],
                     start=True, stop=True).then_inc(s5p, 1)


P_BN, P_M, P_AMAX, P_RS, P_NB = 0, 192, 208, 224, 240
P_TA, P_TB, P_TC, P_R12, P_DS1, P_MX, P_MN = 256, 272, 288, 304, 320, 336, 320
P_TA, P_TB, P_TC = 256, 272, 288
P_R12 = 304
P_DS1 = 320
P_MX = 336
P_MN = 160
P_MN = 360
Q_BN, Q_SS, Q_MXA, Q_RS2, Q_MS, Q_R2, Q_AY2 = 128, 140, 144, 148, 149, 150, 151
Q_V3, Q_R3, Q_NB3, Q_DS2, Q_M3, Q_TA3, Q_TB3, Q_TC3 = 152, 153, 154, 155, 156, 157, 158, 159


def _b6(nc, sems, L):
    g = lambda n: L[n]
    (xres, stt_, g2b, hhalf, zq, thq, dq, uq, tiq, shtile, qkT, ln1T, vpad,
     b1b, b2b, ofc, otile, out_d, ident, ps_acc, ps_a, ps_b, ps_t) = (
        g("xres"), g("stt_"), g("g2b"), g("hhalf"), g("zq"), g("thq"), g("dq"),
        g("uq"), g("tiq"), g("shtile"), g("qkT"), g("ln1T"), g("vpad"),
        g("b1b"), g("b2b"), g("ofc"), g("otile"), g("out_d"), g("ident"),
        g("ps_acc"), g("ps_a"), g("ps_b"), g("ps_t"))
    rstd, ln_merge, n2 = g("rstd"), g("ln_merge"), g("n2")
    MW1, MW2 = W_SC, W_SC + 1

    (s6v, s6a) = sems("s6v", "s6a")
    with nc.Block("b6pre") as blk:
        @blk.vector
        def _(v):
            for i in range(T):
                v.bn_stats(stt_[:, P_BN + i * 12:P_BN + i * 12 + 6],
                           xres[:, i * D:i * D + 512])
                v.bn_stats(stt_[:, P_BN + i * 12 + 6:P_BN + i * 12 + 12],
                           xres[:, i * D + 512:i * D + 1024])
                v.tensor_reduce(stt_[:, P_MX + i:P_MX + i + 1],
                                xres[:, i * D:(i + 1) * D], AX, OP.max)
                v.tensor_reduce(stt_[:, P_MN + i:P_MN + i + 1],
                                xres[:, i * D:(i + 1) * D], AX, OP.min)
            ln_merge(v, T, P_BN, P_M, P_TA, P_TB, P_TC, P_R12)
            n2(v)
            v.scalar_tensor_tensor(stt_[:, P_TB:P_TB + T], stt_[:, P_M:P_M + T],
                                   -1.0, stt_[:, P_MX:P_MX + T], OP.mult, OP.add)
            n2(v)
            v.scalar_tensor_tensor(stt_[:, P_AMAX:P_AMAX + T],
                                   stt_[:, P_MN:P_MN + T], -1.0,
                                   stt_[:, P_M:P_M + T], OP.mult, OP.add)
            n2(v)
            v.tensor_tensor(stt_[:, P_AMAX:P_AMAX + T], stt_[:, P_AMAX:P_AMAX + T],
                            stt_[:, P_TB:P_TB + T], OP.max)
            n2(v)
            v.tensor_scalar(stt_[:, P_TB:P_TB + T], stt_[:, P_TA:P_TA + T], EPS,
                            None, OP.add)
            n2(v)
            v.reciprocal(stt_[:, P_TB:P_TB + T], stt_[:, P_TB:P_TB + T])
            n2(v)
            v.tensor_tensor(stt_[:, P_TB:P_TB + T], stt_[:, P_TA:P_TA + T],
                            stt_[:, P_TB:P_TB + T], OP.mult).then_inc(s6v, 1)
            v.wait_ge(s6a, 1)
            v.tensor_tensor(stt_[:, P_R12:P_R12 + T], stt_[:, P_RS:P_RS + T],
                            stt_[:, P_NB:P_NB + T], OP.mult)
            n2(v)
            v.tensor_tensor(stt_[:, P_TA:P_TA + T], stt_[:, P_AMAX:P_AMAX + T],
                            stt_[:, P_R12:P_R12 + T], OP.mult)
            n2(v)
            v.tensor_scalar(stt_[:, P_TA:P_TA + T], stt_[:, P_TA:P_TA + T],
                            1e-5, None, OP.max)
            n2(v)
            v.reciprocal(stt_[:, P_TB:P_TB + T], stt_[:, P_TA:P_TA + T])
            n2(v)
            v.tensor_tensor(stt_[:, P_TB:P_TB + T], stt_[:, P_TB:P_TB + T],
                            stt_[:, P_R12:P_R12 + T], OP.mult)
            n2(v)
            v.tensor_scalar(stt_[:, P_RS:P_RS + T], stt_[:, P_TB:P_TB + T],
                            127.0, None, OP.mult)
            n2(v)
            v.scalar_tensor_tensor(stt_[:, P_NB:P_NB + T], stt_[:, P_M:P_M + T],
                                   -1.0, stt_[:, P_RS:P_RS + T], OP.mult, OP.mult)
            v.tensor_scalar(stt_[:, P_DS1:P_DS1 + T], stt_[:, P_TA:P_TA + T],
                            -1.0 / 127.0, None, OP.mult)
            n2(v)
            v.tensor_single_scalar(stt_[:, P_DS1:P_DS1 + T],
                                   stt_[:, P_DS1:P_DS1 + T],
                                   stt_[:, MW1:MW1 + 1], OP.mult).then_inc(s6v, 1)

        @blk.scalar
        def _(a):
            a.wait_ge(s6v, 1)
            rstd(a, T, P_TA, P_RS)
            n2(a)
            rstd(a, T, P_TB, P_NB).then_inc(s6a, 1)

    (s7v, s7a, s7p, s7d) = sems("s7v", "s7a", "s7p", "s7d")
    OPS = []
    AC = {"a": 0, "v": 0, "p": 0, "d": 0}

    def em(eng, op, waits=()):
        AC[eng] += 16 if eng == "s" else 1
        OPS.append((eng, op, list(waits), AC[eng]))
        return AC[eng]

    last_tr = 0
    for i in range(_B6T):
        for q in range(5):
            zw = []
            if q == 0:
                if i > 0:
                    zw = [("d", 16 * i), ("v", AC["v"])]
            else:
                zw = [("v", AC["v"])]
            if last_tr:
                zw.append(("p", last_tr))
            a_z = em("a", ("z", i, q), zw)
            v_ti = em("v", ("ti",), [("a", a_z)])
            a_th = em("a", ("th",), [("v", v_ti)])
            v_q = em("v", ("stair",), [("a", a_th)])
            p_tr = em("p", ("tpose", i, q), [("v", v_q)])
            last_tr = p_tr
            a_ev = em("a", ("tev", i, q), [("p", p_tr)])
            if q == 0:
                import os as _os
                NBK = int(_os.environ.get("B6BANKS", "4"))
                for fo in range(8):
                    w = [("a", a_ev)]
                    if fo >= NBK:
                        w.append(("v", AC["v"]))
                    p_m = em("p", ("ffn1", i, fo), w)
                    em("v", ("hev", i, fo), [("p", p_m)])
                    if fo in (3, 7):
                        half = fo // 4
                        for kq in range(2):
                            a_e = em("a", ("erf", half, kq), [("v", AC["v"])])
                            em("v", ("erf1",), [("a", a_e)])
                            em("v", ("g2", half, kq), [])
                for kq in range(4):
                    em("a", ("gsq", kq), [("v", AC["v"])])
                em("v", ("gmax",), [("a", AC["a"])])
                em("v", ("g2pre",), [])
                a_r2 = em("a", ("gr2",), [("v", AC["v"])])
                em("v", ("grs2", i), [("a", a_r2)])
        for do in range(2):
            w = [("a", AC["a"])]
            p_m = em("p", ("ffn2", i, do), w)
            em("v", ("oev", i, do), [("p", p_m)])
        v_st = em("v", ("ostat", i), [])
        a_r3 = em("a", ("or3",), [("v", v_st)])
        v_nb = em("v", ("onb",), [("a", a_r3)])
        a_u = em("a", ("ou", i), [("v", v_nb)])
        v_add = em("v", ("oadd", i), [("a", a_u)])
        AC["d"] += 16
        OPS.append(("s", ("ost", i), [("v", v_add)], AC["d"]))

    semmap = {"a": s7a, "v": s7v, "p": s7p, "d": s7d}

    def do_waits(eng_obj, waits):
        for (e, val) in waits:
            eng_obj.wait_ge(semmap[e], val)

    with nc.Block("b6") as blk:
        @blk.scalar
        def _(a):
            for (eng, op, waits, incv) in OPS:
                if eng != "a":
                    continue
                do_waits(a, waits)
                kind = op[0]
                if kind == "z":
                    _, i, q = op
                    if q == 0:
                        a.activation(zq[:], xres[:, i * D:(i + 1) * D], AF.Identity,
                                     scale=stt_[:, P_RS + i:P_RS + i + 1],
                                     bias=stt_[:, P_NB + i:P_NB + i + 1]
                                     ).then_inc(s7a, 1)
                    else:
                        a.activation(zq[:], g2b[:, (q - 1) * 1024:q * 1024], AF.Identity,
                                     scale=stt_[:, Q_RS2:Q_RS2 + 1]
                                     ).then_inc(s7a, 1)
                elif kind == "th":
                    a.activation(thq[:], tiq[:], AF.Identity,
                                 bias=g("negonec")[:]).then_inc(s7a, 1)
                elif kind == "tev":
                    _, i, q = op
                    cp = None
                    for c in range(8):
                        dst = (shtile[:, c * 128:(c + 1) * 128] if (q == 0 or _B6TEVOFF) else
                               g2b[:, (q - 1) * 1024 + c * 128:
                                   (q - 1) * 1024 + (c + 1) * 128])
                        cp = a.copy(dst, ps_t[:, c * 128:(c + 1) * 128])
                    cp.then_inc(s7a, 1)
                elif kind == "erf":
                    _, half, kq = op
                    a.activation(uq[:], hhalf[:, kq * 1024:(kq + 1) * 1024],
                                 AF.Erf, scale=0.7071067811865476).then_inc(s7a, 1)
                elif kind == "gsq":
                    kq = op[1]
                    a.activation(hhalf[:, 0:1024],
                                 g2b[:, kq * 1024:(kq + 1) * 1024], AF.Square,
                                 accum_out=stt_[:, Q_SS + kq:Q_SS + kq + 1]
                                 ).then_inc(s7a, 1)
                elif kind == "gr2":
                    a.activation(stt_[:, Q_R2:Q_R2 + 1], stt_[:, Q_MS:Q_MS + 1],
                                 AF.Ln, bias=g("epsc")[:])
                    n2(a)
                    a.activation(stt_[:, Q_R2:Q_R2 + 1], stt_[:, Q_R2:Q_R2 + 1],
                                 AF.Exp, scale=-0.5).then_inc(s7a, 1)
                elif kind == "or3":
                    rstd(a, 1, Q_V3, Q_R3).then_inc(s7a, 1)
                elif kind == "ou":
                    a.activation(otile[:], ofc[:], AF.Identity,
                                 scale=stt_[:, Q_R3:Q_R3 + 1],
                                 bias=stt_[:, Q_NB3:Q_NB3 + 1]).then_inc(s7a, 1)

        @blk.vector
        def _(v):
            for (eng, op, waits, incv) in OPS:
                if eng != "v":
                    continue
                do_waits(v, waits)
                kind = op[0]
                if kind == "ti":
                    v.drain()
                    v.tensor_scalar(tiq[:], zq[:], 1.0, None, OP.mult
                                    ).then_inc(s7v, 1)
                elif kind == "stair":
                    v.tensor_tensor(dq[:], zq[:], thq[:], OP.subtract)
                    n2(v)
                    v.scalar_tensor_tensor(uq[:], dq[:], 0.5, thq[:],
                                           OP.is_ge, OP.add)
                    n2(v)
                    v.scalar_tensor_tensor(zq[:], dq[:], 1.5, uq[:],
                                           OP.is_ge, OP.add).then_inc(s7v, 1)
                elif kind == "hev":
                    import os as _os
                    NBK = int(_os.environ.get("B6BANKS", "4"))
                    _, i, fo = op
                    bank = ps_acc[:, (fo % NBK) * 512:(fo % NBK + 1) * 512]
                    v.scalar_tensor_tensor(
                        hhalf[:, (fo % 4) * 512:(fo % 4 + 1) * 512], bank,
                        stt_[:, P_DS1 + i:P_DS1 + i + 1],
                        b1b[:, fo * 512:(fo + 1) * 512],
                        OP.mult, OP.add).then_inc(s7v, 1)
                elif kind == "erf1":
                    v.tensor_scalar(uq[:], uq[:], 1.0, None, OP.add).then_inc(s7v, 1)
                elif kind == "g2":
                    _, half, kq = op
                    n2(v)
                    v.tensor_tensor(
                        g2b[:, half * 2048 + kq * 1024:half * 2048 + (kq + 1) * 1024],
                        hhalf[:, kq * 1024:(kq + 1) * 1024], uq[:],
                        OP.mult).then_inc(s7v, 1)
                elif kind == "gmax":
                    v.drain()
                    v.tensor_reduce(stt_[:, Q_MXA:Q_MXA + 1], g2b[:, 0:2048], AX, OP.max)
                    v.tensor_reduce(stt_[:, Q_MXA + 1:Q_MXA + 2], g2b[:, 2048:4096],
                                    AX, OP.max)
                    v.tensor_reduce(stt_[:, Q_MXA + 2:Q_MXA + 3], g2b[:, 0:2048],
                                    AX, OP.min)
                    v.tensor_reduce(stt_[:, Q_MXA + 3:Q_MXA + 4], g2b[:, 2048:4096],
                                    AX, OP.min).then_inc(s7v, 1)
                elif kind == "g2pre":
                    v.tensor_tensor(stt_[:, Q_MXA:Q_MXA + 1], stt_[:, Q_MXA:Q_MXA + 1],
                                    stt_[:, Q_MXA + 1:Q_MXA + 2], OP.max)
                    v.tensor_tensor(stt_[:, Q_MXA + 2:Q_MXA + 3],
                                    stt_[:, Q_MXA + 2:Q_MXA + 3],
                                    stt_[:, Q_MXA + 3:Q_MXA + 4], OP.min)
                    v.tensor_tensor(stt_[:, Q_SS:Q_SS + 1], stt_[:, Q_SS:Q_SS + 1],
                                    stt_[:, Q_SS + 1:Q_SS + 2], OP.add)
                    v.tensor_tensor(stt_[:, Q_SS + 2:Q_SS + 3],
                                    stt_[:, Q_SS + 2:Q_SS + 3],
                                    stt_[:, Q_SS + 3:Q_SS + 4], OP.add)
                    n2(v)
                    v.scalar_tensor_tensor(stt_[:, Q_MXA:Q_MXA + 1],
                                           stt_[:, Q_MXA + 2:Q_MXA + 3], -1.0,
                                           stt_[:, Q_MXA:Q_MXA + 1], OP.mult, OP.max)
                    v.tensor_tensor(stt_[:, Q_SS:Q_SS + 1], stt_[:, Q_SS:Q_SS + 1],
                                    stt_[:, Q_SS + 2:Q_SS + 3], OP.add)
                    n2(v)
                    v.tensor_scalar(stt_[:, Q_MS:Q_MS + 1], stt_[:, Q_SS:Q_SS + 1],
                                    0.25 / 4096.0, None, OP.mult)
                    v.tensor_scalar(stt_[:, Q_MXA:Q_MXA + 1], stt_[:, Q_MXA:Q_MXA + 1],
                                    0.5, None, OP.mult).then_inc(s7v, 1)
                elif kind == "grs2":
                    v.tensor_tensor(stt_[:, Q_AY2:Q_AY2 + 1], stt_[:, Q_MXA:Q_MXA + 1],
                                    stt_[:, Q_R2:Q_R2 + 1], OP.mult)
                    n2(v)
                    v.tensor_scalar(stt_[:, Q_AY2:Q_AY2 + 1], stt_[:, Q_AY2:Q_AY2 + 1],
                                    1e-5, None, OP.max)
                    n2(v)
                    v.reciprocal(stt_[:, Q_RS2:Q_RS2 + 1], stt_[:, Q_AY2:Q_AY2 + 1])
                    n2(v)
                    v.tensor_tensor(stt_[:, Q_RS2:Q_RS2 + 1], stt_[:, Q_RS2:Q_RS2 + 1],
                                    stt_[:, Q_R2:Q_R2 + 1], OP.mult)
                    n2(v)
                    v.tensor_scalar(stt_[:, Q_RS2:Q_RS2 + 1], stt_[:, Q_RS2:Q_RS2 + 1],
                                    63.5, None, OP.mult)
                    v.tensor_scalar(stt_[:, Q_DS2:Q_DS2 + 1], stt_[:, Q_AY2:Q_AY2 + 1],
                                    -1.0 / 127.0, None, OP.mult)
                    n2(v)
                    v.tensor_single_scalar(stt_[:, Q_DS2:Q_DS2 + 1],
                                           stt_[:, Q_DS2:Q_DS2 + 1],
                                           stt_[:, MW2:MW2 + 1], OP.mult
                                           ).then_inc(s7v, 1)
                elif kind == "oev":
                    _, i, do = op
                    bank = ps_a if do == 0 else ps_b
                    v.scalar_tensor_tensor(ofc[:, do * 512:(do + 1) * 512], bank[:],
                                           stt_[:, Q_DS2:Q_DS2 + 1],
                                           b2b[:, do * 512:(do + 1) * 512],
                                           OP.mult, OP.add).then_inc(s7v, 1)
                elif kind == "ostat":
                    n2(v)
                    v.bn_stats(stt_[:, Q_BN:Q_BN + 6], ofc[:, 0:512])
                    v.bn_stats(stt_[:, Q_BN + 6:Q_BN + 12], ofc[:, 512:1024])
                    ln_merge(v, 1, Q_BN, Q_M3, Q_V3, Q_TA3, Q_TB3, Q_TC3
                             ).then_inc(s7v, 1)
                elif kind == "onb":
                    v.scalar_tensor_tensor(stt_[:, Q_NB3:Q_NB3 + 1],
                                           stt_[:, Q_M3:Q_M3 + 1], -1.0,
                                           stt_[:, Q_R3:Q_R3 + 1], OP.mult, OP.mult
                                           ).then_inc(s7v, 1)
                elif kind == "oadd":
                    i = op[1]
                    v.tensor_tensor(otile[:], otile[:], xres[:, i * D:(i + 1) * D],
                                    OP.add).then_inc(s7v, 1)

        @blk.tensor
        def _(t):
            for (eng, op, waits, incv) in OPS:
                if eng != "p":
                    continue
                do_waits(t, waits)
                kind = op[0]
                if kind == "tpose":
                    mm = None
                    for c in range(8):
                        mm = t.transpose(ps_t[:, c * 128:(c + 1) * 128],
                                         zq[:, c * 128:(c + 1) * 128], ident[:])
                    mm.then_inc(s7p, 1)
                elif kind == "ffn1":
                    import os as _os
                    NBK = int(_os.environ.get("B6BANKS", "4"))
                    _, i, fo = op
                    bank = ps_acc[:, (fo % NBK) * 512:(fo % NBK + 1) * 512]
                    mm = None
                    for dd in range(8):
                        mm = t.matmul(bank, shtile[:, dd * 128:(dd + 1) * 128],
                                      qkT[:, dd * 4096 + fo * 512:
                                          dd * 4096 + (fo + 1) * 512],
                                      start=(dd == 0), stop=(dd == 7))
                    mm.then_inc(s7p, 1)
                elif kind == "ffn2":
                    _, i, do = op
                    bank = ps_a if do == 0 else ps_b
                    mm = None
                    for c in range(32):
                        wsl = (ln1T[:, c * 1024 + do * 512:c * 1024 + (do + 1) * 512]
                               if c < 16 else
                               vpad[:, (c - 16) * 1024 + do * 512:
                                    (c - 16) * 1024 + (do + 1) * 512])
                        mm = t.matmul(bank[:], g2b[:, c * 128:(c + 1) * 128], wsl,
                                      start=(c == 0), stop=(c == 31))
                    mm.then_inc(s7p, 1)

        @blk.sync
        def _(sync):
            for (eng, op, waits, incv) in OPS:
                if eng != "s":
                    continue
                do_waits(sync, waits)
                i = op[1]
                sync.dma_start(out_d[i * 128:(i + 1) * 128, :], otile[:]
                               ).then_inc(s7d, 16)
            sync.wait_ge(s7d, 16 * _B6T)


def _dump(nc, sems, L):
    du = list(_DUMP)
    name, cols, ntiles, coff = du[:4]
    stride = du[4] if len(du) > 4 else cols
    src = L[name] if name in L else L["SB"][name]
    out_d = L["out_d"]
    otile = L["otile"]
    (sdv, sdd) = sems("sdv", "sdd")
    with nc.Block("bdump") as blk:
        @blk.vector
        def _(v):
            for i in range(ntiles):
                if i > 0:
                    v.wait_ge(sdd, 16 * i)
                v.tensor_copy(otile[:, 0:cols],
                              src[:, coff + i * stride:coff + i * stride + cols]
                              ).then_inc(sdv, 1)

        @blk.sync
        def _(sync):
            for i in range(ntiles):
                sync.wait_ge(sdv, i + 1)
                sync.dma_start(out_d[i * 128:(i + 1) * 128, 0:cols],
                               otile[:, 0:cols]).then_inc(sdd, 16)
            sync.wait_ge(sdd, 16 * ntiles)


def _host_inputs(inputs):
    x = np.ascontiguousarray(np.asarray(inputs["L0_attn_hidden_states"]))
    w1t = np.ascontiguousarray(np.asarray(inputs["L0_ffn_w1"]).T)
    w2t = np.ascontiguousarray(np.asarray(inputs["L0_ffn_w2"]).T)
    ident = np.eye(128, dtype=ml_dtypes.bfloat16)
    ii = np.arange(128)
    cmaskv = np.where(ii[:, None] > ii[None, :], np.float32(NEGBIG),
                      np.float32(0.0)).astype(np.float32)
    onesr = np.ones((1, 128), dtype=ml_dtypes.bfloat16)
    base = dict(
        w_attn=np.ascontiguousarray(np.asarray(inputs["L0_attn_w_attn"])),
        b_attn=np.ascontiguousarray(np.asarray(inputs["L0_attn_b_attn"])),
        w_proj=np.ascontiguousarray(np.asarray(inputs["L0_attn_w_proj"])),
        b_proj=np.ascontiguousarray(np.asarray(inputs["L0_attn_b_proj"])),
        w1t=w1t, b1=np.ascontiguousarray(np.asarray(inputs["L0_ffn_b1"])),
        w2t=w2t, b2=np.ascontiguousarray(np.asarray(inputs["L0_ffn_b2"])),
        idm=ident, cmask=cmaskv, onesr=onesr,
    )
    return [dict(base, x=np.ascontiguousarray(x[c])) for c in range(8)]


def kernel(**inputs):
    in_maps = _host_inputs(inputs)
    nc, _ = build_nc()
    res = run_bass_kernel_spmd(nc, in_maps, core_ids=list(range(8)))
    out = np.stack([np.asarray(r["out"]) for r in res.results], axis=0)
    return out.astype(np.float32)


# version 02 (final)
# speedup vs baseline: 1.0298x; 1.0298x over previous
import numpy as np
import ml_dtypes
import concourse.bass as bass
import concourse.mybir as mybir
from concourse.bass_utils import run_bass_kernel_spmd
from concourse.alu_op_type import AluOpType as OP

F32 = mybir.dt.float32
BF16 = mybir.dt.bfloat16
I16 = mybir.dt.int16
AF = mybir.ActivationFunctionType
AX = mybir.AxisListType.X

S, D, T, H, HD, F = 2048, 1024, 16, 16, 64, 4096
EPS = 1e-5
NEGBIG = -1.0e38

_UPTO = 99
_B3H = 16
_B3NORM = True
_B3LVL = 4
_B6T = 16
_B6TEVOFF = False
_DUMP = None


def _attn_pieces():
    pieces = []
    for j in range(T):
        qc = j * 128
        first = True
        while qc < S:
            w = min(512 - (qc % 512), S - qc)
            pieces.append(dict(j=j, qc=qc, w=w, diag=first,
                               stop=(first and j % 4 == 3)))
            qc += w
            first = False
    return pieces


def build_nc():
    nc = bass.Bass("TRN2", target_bir_lowering=False,
                   detect_race_conditions=False)

    x_d = nc.dram_tensor("x", [S, D], F32, kind="ExternalInput").ap()
    wat_d = nc.dram_tensor("w_attn", [D, 3 * D], F32, kind="ExternalInput").ap()
    bat_d = nc.dram_tensor("b_attn", [3 * D], F32, kind="ExternalInput").ap()
    wpr_d = nc.dram_tensor("w_proj", [D, D], F32, kind="ExternalInput").ap()
    bpr_d = nc.dram_tensor("b_proj", [D], F32, kind="ExternalInput").ap()
    w1t_d = nc.dram_tensor("w1t", [D, F], F32, kind="ExternalInput").ap()
    b1_d = nc.dram_tensor("b1", [F], F32, kind="ExternalInput").ap()
    w2t_d = nc.dram_tensor("w2t", [F, D], F32, kind="ExternalInput").ap()
    b2_d = nc.dram_tensor("b2", [D], F32, kind="ExternalInput").ap()
    idm_d = nc.dram_tensor("idm", [128, 128], BF16, kind="ExternalInput").ap()
    cm_d = nc.dram_tensor("cmask", [128, 128], F32, kind="ExternalInput").ap()
    on_d = nc.dram_tensor("onesr", [1, 128], BF16, kind="ExternalInput").ap()
    out_d = nc.dram_tensor("out", [S, D], F32, kind="ExternalOutput").ap()

    _BASE = (nc.sbuf_base + 31) & ~31

    def at(name, shape, dtype, offset):
        return nc.alloc_sbuf_tensor_at(name, shape, dtype, offset=offset + _BASE)
    xres = at("xres", [128, T * D], BF16, offset=0)
    qkT = at("qkT", [128, 16 * S], BF16, offset=32768)
    ln1T = at("ln1T", [128, 8 * S], BF16, offset=98304)
    vpad = at("vpad", [128, T * 1040], BF16, offset=131072)
    R4 = 164352
    rowf = at("rowf", [1, 4096], F32, offset=R4)
    rowb = at("rowb", [1, 4096], BF16, offset=180736)
    wqk = at("wqk", [128, 2 * 4096], BF16, offset=R4)
    wv = at("wv", [128, 8 * 1024], BF16, offset=R4)
    wproj = at("wproj", [128, 8 * 1024], BF16, offset=R4)
    wstr = at("wstr", [128, 4096], F32, offset=R4)
    rrow = at("rrow", [1, 2048], F32, offset=R4)
    rrowb = at("rrowb", [1, 2048], BF16, offset=R4 + 8192)
    rb = at("rb", [128, 512], BF16, offset=R4 + 12288)
    hhalf = at("hhalf", [128, 2048], F32, offset=R4)
    g2b = at("g2b", [128, 4096], BF16, offset=R4 + 8192)
    R5 = 180736
    probs = at("probs", [128, 2 * 2048], BF16, offset=R5)
    tln = at("tln", [128, D], F32, offset=R5)
    uln = at("uln", [128, D], F32, offset=R5 + 4096)
    wqs = at("wqs", [128, 2048], F32, offset=R5)
    zq = at("zq", [128, 1024], BF16, offset=R5)
    thq = at("thq", [128, 1024], BF16, offset=R5 + 2048)
    otile = at("otile", [128, D], F32, offset=R5)
    dq = at("dq", [128, 1024], BF16, offset=R5 + 4096)
    uq = at("uq", [128, 1024], BF16, offset=R5 + 6144)
    ofc = at("ofc", [128, D], F32, offset=R5 + 4096)
    b1b = at("b1b", [128, F], BF16, offset=188928)
    b2b = at("b2b", [128, D], BF16, offset=197120)
    bvb = at("bvb", [128, D], BF16, offset=199168)
    bprb = at("bprb", [128, D], BF16, offset=201216)
    ident = at("ident", [128, 128], BF16, offset=203264)
    cmask = at("cmask_s", [128, 128], F32, offset=203520)
    onesr = at("onesr_s", [1, 128], BF16, offset=204032)
    onesc = at("onesc", [128, 1], F32, offset=204288)
    batfm = at("batfm", [128, 16], F32, offset=204320)
    stt_ = at("stats", [128, 400], F32, offset=204384)
    shtile = at("shtile", [128, 1024], BF16, offset=205984)
    tiq = at("tiq", [128, 1024], I16, offset=208032)
    mrowf = at("mrowf", [1, 16], F32, offset=210080)
    onesf = at("onesf", [1, 128], F32, offset=210272)
    mrow = at("mrow", [1, 16], BF16, offset=210144)
    epsc = at("epsc", [128, 1], F32, offset=210208)
    negonec = at("negonec", [128, 1], F32, offset=210240)

    ps_acc = nc.alloc_psum_tensor("ps_acc", [128, 2048], F32)
    ps_a = nc.alloc_psum_tensor("ps_a", [128, 512], F32)
    ps_b = nc.alloc_psum_tensor("ps_b", [128, 512], F32)
    ps_c = nc.alloc_psum_tensor("ps_c", [128, 512], F32)
    ps_t = nc.alloc_psum_tensor("ps_t", [128, 1024], BF16)

    SB = dict(xres=xres, qkT=qkT, ln1T=ln1T, vpad=vpad, stt_=stt_, g2b=g2b)

    def sems(*names):
        return [nc.alloc_semaphore(n) for n in names]

    def n2(v):
        v.drain()

    def ln_merge(v, nt, bco, mco, varco, tA, tB, tC):
        st = stt_[:, bco:bco + nt * 12].rearrange("p (t c) -> p t c", c=12)
        mA, mB, mC, mD_ = st[:, :, 1], st[:, :, 4], st[:, :, 7], st[:, :, 10]
        vA, vB, vC, vD = st[:, :, 2], st[:, :, 5], st[:, :, 8], st[:, :, 11]
        M = stt_[:, mco:mco + nt]
        VAR = stt_[:, varco:varco + nt]
        TA = stt_[:, tA:tA + nt]
        TB = stt_[:, tB:tB + nt]
        TC = stt_[:, tC:tC + nt]
        v.drain()
        v.tensor_tensor(TA, mA, mB, OP.add)
        v.tensor_tensor(TB, mC, mD_, OP.add)
        v.tensor_tensor(VAR, vA, vB, OP.add)
        v.tensor_tensor(TC, vC, vD, OP.add)
        n2(v)
        v.tensor_tensor(M, TA, TB, OP.add)
        v.tensor_tensor(VAR, VAR, TC, OP.add)
        n2(v)
        v.tensor_scalar(M, M, 0.25, None, OP.mult)
        n2(v)
        v.scalar_tensor_tensor(TA, M, -1.0, mA, OP.mult, OP.add)
        n2(v)
        v.tensor_tensor(TA, TA, TA, OP.mult)
        n2(v)
        last = None
        for mm_ in (mB, mC, mD_):
            v.scalar_tensor_tensor(TB, M, -1.0, mm_, OP.mult, OP.add)
            n2(v)
            v.tensor_tensor(TB, TB, TB, OP.mult)
            n2(v)
            v.tensor_tensor(TA, TA, TB, OP.add)
            n2(v)
        v.tensor_scalar(TA, TA, 0.25, None, OP.mult)
        n2(v)
        last = v.scalar_tensor_tensor(VAR, VAR, 1.0 / 1024.0, TA, OP.mult, OP.add)
        return last

    def rstd(a, nt, varco, rco):
        a.activation(stt_[:, rco:rco + nt], stt_[:, varco:varco + nt], AF.Ln,
                     bias=epsc[:])
        n2(a)
        return a.activation(stt_[:, rco:rco + nt], stt_[:, rco:rco + nt], AF.Exp,
                            scale=-0.5)

    (s0d, s0g) = sems("s0d", "s0g")
    with nc.Block("b0a") as blk:
        @blk.sync
        def _(sync):
            sync.dma_start(ident[:], idm_d[:]).then_inc(s0d, 16)
            sync.dma_start(cmask[:], cm_d[:]).then_inc(s0d, 16)
            sync.dma_start(onesr[:], on_d[:]).then_inc(s0d, 16)

        @blk.gpsimd
        def _(g):
            xg = x_d.rearrange("(g t p) d -> g p t d", p=128, t=4)
            for gi in range(4):
                g.dma_start(xres[:, gi * 4096:(gi + 1) * 4096], xg[gi]).then_inc(s0g, 16)
            for c in range(16):
                g.dma_start(batfm[:, c:c + 1],
                            bat_d[c * 128:(c + 1) * 128].rearrange("(a p) -> p a", p=128)
                            ).then_inc(s0g, 16)

        @blk.vector
        def _(v):
            v.memset(onesc[:], 1.0)
            v.memset(epsc[:], EPS)
            v.memset(onesf[:], 1.0)
            v.memset(negonec[:], -1.0)
            for j in range(T):
                ap = vpad[:, j * 1040:j * 1040 + 1040] \
                    .rearrange("p (h c) -> p h c", c=65)[:, :, 64:65]
                v.memset(ap, 1.0)
            v.wait_ge(s0g, 16 * 20)
            v.wait_ge(s0d, 16 * 3)

    for bname, src, width, dstb in [("b1", b1_d[:], F, b1b),
                                    ("b2", b2_d[:], D, b2b),
                                    ("bv", bat_d[2048:3072], D, bvb),
                                    ("bp", bpr_d[:], D, bprb)]:
        (sd, sv, sp) = sems(f"s0{bname}d", f"s0{bname}v", f"s0{bname}p")
        npc = (width + 511) // 512
        with nc.Block("b0" + bname) as blk:
            @blk.sync
            def _(sync, src=src, width=width, sd=sd):
                sync.dma_start(rowf[:1, 0:width], src).then_inc(sd, 16)

            @blk.vector
            def _(v, width=width, dstb=dstb, sd=sd, sv=sv, sp=sp, npc=npc):
                v.wait_ge(sd, 16)
                v.tensor_copy(rowb[:1, 0:width], rowf[:1, 0:width]).then_inc(sv, 1)
                for p in range(npc):
                    w = min(512, width - p * 512)
                    v.wait_ge(sp, p + 1)
                    v.tensor_copy(dstb[:, p * 512:p * 512 + w], ps_c[:, 0:w]
                                  ).then_inc(sv, 1)

            @blk.tensor
            def _(t, width=width, sv=sv, sp=sp, npc=npc):
                t.wait_ge(sv, 1)
                for p in range(npc):
                    w = min(512, width - p * 512)
                    if p > 0:
                        t.wait_ge(sv, 1 + p)
                    t.matmul(ps_c[0:128, 0:w], onesr[:1, :],
                             rowb[:1, p * 512:p * 512 + w],
                             start=True, stop=True).then_inc(sp, 1)

    if _UPTO >= 1:
        _b1(nc, sems, locals())
    if _UPTO >= 2:
        _b2(nc, sems, locals())
    if _UPTO >= 3:
        _b3(nc, sems, locals())
    if _UPTO >= 4:
        _b4(nc, sems, locals())
    if _UPTO >= 5:
        _b5(nc, sems, locals())
    if _UPTO >= 6:
        _b6(nc, sems, locals())
    if _DUMP is not None:
        _dump(nc, sems, locals())
    return nc, SB


C_BN, C_M, C_R, C_NB, C_V, C_TA, C_TB, C_TC = 0, 192, 208, 224, 240, 256, 272, 288
W_SC = 376


def _b1(nc, sems, L):
    g = lambda n: L[n]
    stt_, xres, shtile, ln1T, ident, ps_t = (
        g("stt_"), g("xres"), g("shtile"), g("ln1T"), g("ident"), g("ps_t"))
    rstd, ln_merge, n2 = g("rstd"), g("ln_merge"), g("n2")
    (s1v, s1a, s1p) = sems("s1v", "s1a", "s1p")
    with nc.Block("b1") as blk:
        @blk.vector
        def _(v):
            for i in range(T):
                v.bn_stats(stt_[:, C_BN + i * 12:C_BN + i * 12 + 6],
                           xres[:, i * D:i * D + 512])
                v.bn_stats(stt_[:, C_BN + i * 12 + 6:C_BN + i * 12 + 12],
                           xres[:, i * D + 512:i * D + 1024])
            ln_merge(v, T, C_BN, C_M, C_V, C_TA, C_TB, C_TC).then_inc(s1v, 1)
            v.wait_ge(s1a, 1)
            v.scalar_tensor_tensor(stt_[:, C_NB:C_NB + T], stt_[:, C_M:C_M + T],
                                   -1.0, stt_[:, C_R:C_R + T], OP.mult, OP.mult
                                   ).then_inc(s1v, 1)
            for i in range(T):
                v.wait_ge(s1p, i + 1)
                cp = None
                for c in range(8):
                    cp = v.tensor_copy(ln1T[:, c * S + i * 128:c * S + (i + 1) * 128],
                                       ps_t[:, c * 128:(c + 1) * 128])
                cp.then_inc(s1v, 1)

        @blk.scalar
        def _(a):
            a.wait_ge(s1v, 1)
            rstd(a, T, C_V, C_R).then_inc(s1a, 1)
            a.wait_ge(s1v, 2)
            for i in range(T):
                if i > 0:
                    a.wait_ge(s1p, i)
                a.activation(shtile[:], xres[:, i * D:(i + 1) * D], AF.Identity,
                             scale=stt_[:, C_R + i:C_R + i + 1],
                             bias=stt_[:, C_NB + i:C_NB + i + 1]).then_inc(s1a, 1)

        @blk.tensor
        def _(t):
            for i in range(T):
                t.wait_ge(s1a, i + 2)
                if i > 0:
                    t.wait_ge(s1v, 2 + i)
                mm = None
                for c in range(8):
                    mm = t.transpose(ps_t[:, c * 128:(c + 1) * 128],
                                     shtile[:, c * 128:(c + 1) * 128], ident[:])
                mm.then_inc(s1p, 1)


def _b2(nc, sems, L):
    g = lambda n: L[n]
    (wat_d, wqk, wv, ln1T, qkT, vpad, batfm, bvb, ps_acc, ps_a, ps_b) = (
        g("wat_d"), g("wqk"), g("wv"), g("ln1T"), g("qkT"), g("vpad"),
        g("batfm"), g("bvb"), g("ps_acc"), g("ps_a"), g("ps_b"))
    (s2g, s2p, s2a, s2v) = sems("s2g", "s2p", "s2a", "s2v")
    with nc.Block("b2") as blk:
        @blk.gpsimd
        def _(gp):
            for gi in range(4):
                if gi >= 2:
                    gp.wait_ge(s2p, 4 * (gi - 1))
                for d in range(8):
                    gp.dma_start(
                        wqk[:, (gi % 2) * 4096 + d * 512:(gi % 2) * 4096 + (d + 1) * 512],
                        wat_d[d * 128:(d + 1) * 128, gi * 512:(gi + 1) * 512]
                    ).then_inc(s2g, 16)
            gp.wait_ge(s2p, 16)
            for d in range(8):
                gp.dma_start(wv[:, d * 1024:(d + 1) * 1024],
                             wat_d[d * 128:(d + 1) * 128, 2048:3072]).then_inc(s2g, 16)

        @blk.tensor
        def _(t):
            for gi in range(4):
                t.wait_ge(s2g, 16 * 8 * (gi + 1))
                for sp in range(4):
                    idx = gi * 4 + sp
                    if idx > 0:
                        t.wait_ge(s2a, idx)
                    mm = None
                    for fi in range(4):
                        for d in range(8):
                            mm = t.matmul(
                                ps_acc[:, fi * 512:(fi + 1) * 512],
                                wqk[:, (gi % 2) * 4096 + d * 512 + fi * 128:
                                    (gi % 2) * 4096 + d * 512 + (fi + 1) * 128],
                                ln1T[:, d * S + sp * 512:d * S + (sp + 1) * 512],
                                start=(d == 0), stop=(d == 7))
                    mm.then_inc(s2p, 1)
            t.wait_ge(s2g, 16 * 40)
            for i in range(T):
                for fh in range(2):
                    k = 2 * i + fh
                    if k >= 2:
                        t.wait_ge(s2v, k - 1)
                    bank = ps_a if fh == 0 else ps_b
                    mm = None
                    for d in range(8):
                        mm = t.matmul(bank[:],
                                      ln1T[:, d * S + i * 128:d * S + (i + 1) * 128],
                                      wv[:, d * 1024 + fh * 512:d * 1024 + (fh + 1) * 512],
                                      start=(d == 0), stop=(d == 7))
                    mm.then_inc(s2p, 1)

        @blk.scalar
        def _(a):
            for gi in range(4):
                for sp in range(4):
                    idx = gi * 4 + sp
                    a.wait_ge(s2p, idx + 1)
                    cp = None
                    for fi in range(4):
                        c = gi * 4 + fi
                        cp = a.activation(
                            qkT[:, c * S + sp * 512:c * S + (sp + 1) * 512],
                            ps_acc[:, fi * 512:(fi + 1) * 512], AF.Identity,
                            bias=batfm[:, c:c + 1])
                    cp.then_inc(s2a, 1)

        @blk.vector
        def _(v):
            for i in range(T):
                for fh in range(2):
                    k = 2 * i + fh
                    v.wait_ge(s2p, 16 + k + 1)
                    bank = ps_a if fh == 0 else ps_b
                    dst = vpad[:, i * 1040 + fh * 520:i * 1040 + (fh + 1) * 520] \
                        .rearrange("p (h c) -> p h c", c=65)[:, :, 0:64]
                    v.tensor_tensor(dst,
                                    bank[:].rearrange("p (h c) -> p h c", c=64),
                                    bvb[:, fh * 512:(fh + 1) * 512]
                                    .rearrange("p (h c) -> p h c", c=64),
                                    OP.add).then_inc(s2v, 1)


def _b3(nc, sems, L):
    g = lambda n: L[n]
    (qkT, vpad, probs, cmask, rrow, rrowb, rb, onesr, ps_acc, ps_a, ps_b, ps_c) = (
        g("qkT"), g("vpad"), g("probs"), g("cmask"), g("rrow"), g("rrowb"),
        g("rb"), g("onesr"), g("ps_acc"), g("ps_a"), g("ps_b"), g("ps_c"))
    n2 = g("n2")
    pieces = _attn_pieces()
    NP = len(pieces)
    (s3p, s3m, s3e, s3q, s3v, s3c, s3n, s3r) = sems(
        "s3p", "s3m", "s3e", "s3q", "s3v", "s3c", "s3n", "s3r")
    cumj = []
    cc = 0
    for j in range(T):
        cc += len([p for p in pieces if p["j"] == j])
        cumj.append(cc)
    diag_ids = [k for k, p in enumerate(pieces) if p["diag"]]
    nd = len(diag_ids)
    mask_of = {k: i for i, k in enumerate(diag_ids)}

    def kslice(h, j):
        c = 8 + h // 2
        r0 = (h % 2) * 64
        return qkT[r0:r0 + 64, c * S + j * 128:c * S + (j + 1) * 128]

    def qslice(h, qc, w):
        c = h // 2
        r0 = (h % 2) * 64
        return qkT[r0:r0 + 64, c * S + qc:c * S + qc + w]

    by_j = {}
    for k, p in enumerate(pieces):
        by_j.setdefault(p["j"], []).append((k, p))

    with nc.Block("b3") as blk:
        @blk.tensor
        def _(t):
            for h in range(_B3H):
                def emit_pv(jj):
                    for k, p in by_j[jj]:
                        gid = h * NP + k
                        t.wait_ge(s3e, gid + 1)
                        if p["j"] == 0 and h > 0:
                            t.wait_ge(s3v, h * 6)
                        if _B3LVL < 4:
                            t.nop().then_inc(s3q, 1)
                            continue
                        po = (p["j"] % 2) * 2048 + (p["qc"] - p["j"] * 128)
                        t.matmul(ps_acc[0:65, p["qc"]:p["qc"] + p["w"]],
                                 vpad[:, p["j"] * 1040 + h * 65:
                                      p["j"] * 1040 + h * 65 + 65],
                                 probs[:, po:po + p["w"]],
                                 start=(p["j"] == 0), stop=p["stop"]
                                 ).then_inc(s3q, 1)

                for jj in range(T):
                    for k, p in by_j[jj]:
                        gid = h * NP + k
                        if gid >= 2:
                            t.wait_ge(s3e, gid - 1)
                        bank = ps_a if gid % 2 == 0 else ps_b
                        if _B3LVL < 1:
                            t.nop().then_inc(s3p, 1)
                        else:
                            t.matmul(bank[:, 0:p["w"]], kslice(h, p["j"]),
                                     qslice(h, p["qc"], p["w"]),
                                     start=True, stop=True).then_inc(s3p, 1)
                    if jj > 0:
                        emit_pv(jj - 1)
                emit_pv(T - 1)
                if not _B3NORM:
                    continue
                t.wait_ge(s3v, h * 6 + 2)
                for b in range(4):
                    if b > 0:
                        t.wait_ge(s3c, h * 4 + b)
                    t.matmul(ps_c[0:64, 0:512], onesr[:1, 0:64],
                             rrowb[:1, b * 512:(b + 1) * 512],
                             start=True, stop=True).then_inc(s3n, 1)

        @blk.vector
        def _(v):
            for h in range(_B3H):
                for jj in range(T):
                    k = diag_ids[jj]
                    gid = h * NP + k
                    v.wait_ge(s3p, gid + 1)
                    bank = ps_a if gid % 2 == 0 else ps_b
                    if _B3LVL < 2:
                        v.nop().then_inc(s3m, 1)
                    else:
                        v.tensor_tensor(bank[:, 0:128], bank[:, 0:128], cmask[:],
                                        OP.add).then_inc(s3m, 1)
                    if jj >= 2:
                        v.wait_ge(s3q, h * NP + cumj[jj - 2])
                        v.nop().then_inc(s3r, 1)
                v.wait_ge(s3q, (h + 1) * NP)
                if not _B3NORM:
                    for _ in range(6):
                        v.nop().then_inc(s3v, 1)
                    continue
                v.reciprocal(rrow[:1, :], ps_acc[64:65, 0:2048]).then_inc(s3v, 1)
                n2(v)
                v.tensor_copy(rrowb[:1, :], rrow[:1, :]).then_inc(s3v, 1)
                for b in range(4):
                    v.wait_ge(s3c, h * 4 + b + 1)
                    r0 = (h % 2) * 64
                    dst = qkT[r0:r0 + 64,
                              (h // 2) * S + b * 512:(h // 2) * S + (b + 1) * 512]
                    v.tensor_tensor(dst, ps_acc[0:64, b * 512:(b + 1) * 512],
                                    rb[0:64, :], OP.mult).then_inc(s3v, 1)

        @blk.scalar
        def _(a):
            for h in range(_B3H):
                for k, p in enumerate(pieces):
                    gid = h * NP + k
                    if p["diag"]:
                        a.wait_ge(s3m, h * nd + mask_of[k] + 1)
                    else:
                        a.wait_ge(s3p, gid + 1)
                    if p["j"] >= 2:
                        a.wait_ge(s3r, h * (T - 2) + p["j"] - 1)
                    bank = ps_a if gid % 2 == 0 else ps_b
                    po = (p["j"] % 2) * 2048 + (p["qc"] - p["j"] * 128)
                    if _B3LVL < 3:
                        a.nop().then_inc(s3e, 1)
                    else:
                        a.activation(probs[:, po:po + p["w"]],
                                     bank[:, 0:p["w"]], AF.Exp).then_inc(s3e, 1)
                if not _B3NORM:
                    continue
                for b in range(4):
                    a.wait_ge(s3n, h * 4 + b + 1)
                    a.wait_ge(s3v, h * 6 + 2 + b)
                    a.copy(rb[0:64, :], ps_c[0:64, 0:512]).then_inc(s3c, 1)


def _b4(nc, sems, L):
    g = lambda n: L[n]
    (wpr_d, wproj, qkT, bprb, tln, uln, xres, stt_, ps_a, ps_b) = (
        g("wpr_d"), g("wproj"), g("qkT"), g("bprb"), g("tln"), g("uln"),
        g("xres"), g("stt_"), g("ps_a"), g("ps_b"))
    rstd, ln_merge, n2 = g("rstd"), g("ln_merge"), g("n2")
    (s4g, s4p, s4v, s4a) = sems("s4g", "s4p", "s4v", "s4a")
    BN, M_, V_, R_, NB_, TA_, TB_, TC_ = 0, 12, 13, 14, 15, 16, 17, 18
    with nc.Block("b4") as blk:
        @blk.gpsimd
        def _(gp):
            for d in range(8):
                gp.dma_start(wproj[:, d * 1024:(d + 1) * 1024],
                             wpr_d[d * 128:(d + 1) * 128, :]).then_inc(s4g, 16)

        @blk.tensor
        def _(t):
            t.wait_ge(s4g, 16 * 8)
            for i in range(T):
                for eh in range(2):
                    if i > 0:
                        t.wait_ge(s4v, 5 * (i - 1) + eh + 1)
                    bank = ps_a if eh == 0 else ps_b
                    mm = None
                    for d in range(8):
                        mm = t.matmul(bank[:],
                                      qkT[:, d * S + i * 128:d * S + (i + 1) * 128],
                                      wproj[:, d * 1024 + eh * 512:d * 1024 + (eh + 1) * 512],
                                      start=(d == 0), stop=(d == 7))
                    mm.then_inc(s4p, 1)

        @blk.vector
        def _(v):
            for i in range(T):
                for eh in range(2):
                    v.wait_ge(s4p, 2 * i + eh + 1)
                    bank = ps_a if eh == 0 else ps_b
                    v.tensor_tensor(tln[:, eh * 512:(eh + 1) * 512], bank[:],
                                    bprb[:, eh * 512:(eh + 1) * 512],
                                    OP.add).then_inc(s4v, 1)
                if i > 0:
                    v.wait_ge(s4a, 2 * i)
                v.drain()
                v.bn_stats(stt_[:, BN:BN + 6], tln[:, 0:512])
                v.bn_stats(stt_[:, BN + 6:BN + 12], tln[:, 512:1024])
                ln_merge(v, 1, BN, M_, V_, TA_, TB_, TC_).then_inc(s4v, 1)
                v.wait_ge(s4a, 2 * i + 1)
                v.scalar_tensor_tensor(stt_[:, NB_:NB_ + 1], stt_[:, M_:M_ + 1],
                                       -1.0, stt_[:, R_:R_ + 1], OP.mult, OP.mult
                                       ).then_inc(s4v, 1)
                v.wait_ge(s4a, 2 * i + 2)
                v.tensor_tensor(xres[:, i * D:(i + 1) * D], xres[:, i * D:(i + 1) * D],
                                uln[:], OP.add).then_inc(s4v, 1)

        @blk.scalar
        def _(a):
            for i in range(T):
                a.wait_ge(s4v, 5 * i + 3)
                rstd(a, 1, V_, R_).then_inc(s4a, 1)
                a.wait_ge(s4v, 5 * i + 4)
                a.activation(uln[:], tln[:], AF.Identity,
                             scale=stt_[:, R_:R_ + 1],
                             bias=stt_[:, NB_:NB_ + 1]).then_inc(s4a, 1)


def _b5(nc, sems, L):
    g = lambda n: L[n]
    (w1t_d, w2t_d, wstr, wqs, qkT, ln1T, vpad, stt_, onesc, onesr,
     mrowf, mrow, ps_c, onesf) = (
        g("w1t_d"), g("w2t_d"), g("wstr"), g("wqs"), g("qkT"), g("ln1T"),
        g("vpad"), g("stt_"), g("onesc"), g("onesr"), g("mrowf"), g("mrow"),
        g("ps_c"), g("onesf"))
    n2 = g("n2")
    (s5d, s5v, s5a, s5p, s5r) = sems("s5d", "s5v", "s5a", "s5p", "s5r")
    W1S, W2S = 0, 32
    SUM1, SUM2 = W_SC + 6, W_SC + 7
    MW1, MW2 = W_SC, W_SC + 1
    THR1, THR2 = W_SC + 2, W_SC + 3
    NTH1, NTH2 = W_SC + 4, W_SC + 5
    w1p = [(d, q) for d in range(8) for q in range(4)]

    def w1src(k):
        d, q = w1p[k]
        return w1t_d[d * 128:(d + 1) * 128, q * 1024:(q + 1) * 1024]

    def w2src(k):
        return w2t_d[k * 128:(k + 1) * 128, :]

    with nc.Block("b5") as blk:
        @blk.sync
        def _(sync):
            for k in range(64):
                if k >= 4:
                    sync.wait_ge(s5a, k - 3)
                sync.dma_start(wstr[:, (k % 4) * 1024:(k % 4) * 1024 + 1024],
                               w1src(k) if k < 32 else w2src(k - 32)
                               ).then_inc(s5d, 16)
            sync.wait_ge(s5v, 6)
            for k in range(64):
                if k >= 4:
                    sync.wait_ge(s5v, 6 + 2 * (k - 3))
                sync.dma_start(wstr[:, (k % 4) * 1024:(k % 4) * 1024 + 1024],
                               w1src(k) if k < 32 else w2src(k - 32)
                               ).then_inc(s5d, 16)

        @blk.scalar
        def _(a):
            for k in range(64):
                a.wait_ge(s5d, 16 * (k + 1))
                if k >= 2:
                    a.wait_ge(s5r, k - 1)
                a.activation(wqs[:, (k % 2) * 1024:(k % 2) * 1024 + 1024],
                             wstr[:, (k % 4) * 1024:(k % 4) * 1024 + 1024],
                             AF.Abs).then_inc(s5a, 1)

        @blk.vector
        def _(v):
            for k in range(64):
                v.wait_ge(s5a, k + 1)
                col = W1S + k if k < 32 else W2S + (k - 32)
                v.tensor_reduce(stt_[:, col:col + 1],
                                wqs[:, (k % 2) * 1024:(k % 2) * 1024 + 1024],
                                AX, OP.add).then_inc(s5r, 1)
            v.drain()
            v.tensor_reduce(stt_[:, SUM1:SUM1 + 1], stt_[:, W1S:W1S + 32], AX,
                            OP.add).then_inc(s5v, 1)
            v.drain()
            v.tensor_reduce(stt_[:, SUM2:SUM2 + 1], stt_[:, W2S:W2S + 32], AX,
                            OP.add).then_inc(s5v, 1)
            v.wait_ge(s5p, 2)
            v.tensor_scalar(mrowf[:1, 0:2], ps_c[0:1, 0:2], 1.0 / (D * F), 1e-5,
                            OP.mult, OP.max).then_inc(s5v, 1)
            v.drain()
            v.nop().then_inc(s5v, 1)
            v.wait_ge(s5p, 3)
            v.tensor_copy(stt_[:, MW1:MW1 + 2], ps_c[:, 64:66])
            v.drain()
            v.tensor_scalar(stt_[:, THR1:THR1 + 2], stt_[:, MW1:MW1 + 2],
                            0.5, None, OP.mult).then_inc(s5v, 1)
            v.drain()
            v.tensor_scalar(stt_[:, NTH1:NTH1 + 2], stt_[:, THR1:THR1 + 2],
                            -1.0, None, OP.mult).then_inc(s5v, 1)
            for k in range(64):
                v.wait_ge(s5d, 16 * (64 + k + 1))
                v.drain()
                src_ = wstr[:, (k % 4) * 1024:(k % 4) * 1024 + 1024]
                cmp_ = wqs[:, (k % 2) * 1024:(k % 2) * 1024 + 1024]
                if k < 32:
                    d, q = w1p[k]
                    dst = qkT[:, d * 4096 + q * 1024:d * 4096 + (q + 1) * 1024]
                    thr_, nth_ = THR1, NTH1
                else:
                    kk = k - 32
                    dst = (ln1T[:, kk * 1024:(kk + 1) * 1024] if kk < 16
                           else vpad[:, (kk - 16) * 1024:(kk - 15) * 1024])
                    thr_, nth_ = THR2, NTH2
                v.tensor_single_scalar(cmp_, src_, stt_[:, thr_:thr_ + 1],
                                       OP.is_gt).then_inc(s5v, 1)
                v.drain()
                v.scalar_tensor_tensor(dst, src_, stt_[:, nth_:nth_ + 1],
                                       cmp_, OP.is_lt, OP.subtract
                                       ).then_inc(s5v, 1)

        @blk.tensor
        def _(t):
            for wi in range(2):
                t.wait_ge(s5v, wi + 1)
                t.matmul(ps_c[0:1, wi:wi + 1], stt_[:, SUM1 + wi:SUM1 + wi + 1],
                         onesc[:], start=True, stop=True).then_inc(s5p, 1)
            t.wait_ge(s5v, 4)
            t.matmul(ps_c[0:128, 64:66], onesf[:1, :], mrowf[:1, 0:2],
                     start=True, stop=True).then_inc(s5p, 1)


P_BN, P_M, P_AMAX, P_RS, P_NB = 0, 192, 208, 224, 240
P_TA, P_TB, P_TC, P_R12, P_DS1, P_MX, P_MN = 256, 272, 288, 304, 320, 336, 320
P_TA, P_TB, P_TC = 256, 272, 288
P_R12 = 304
P_DS1 = 320
P_MX = 336
P_MN = 160
P_MN = 360
Q_BN, Q_SS, Q_MXA, Q_RS2, Q_MS, Q_R2, Q_AY2 = 128, 140, 144, 148, 149, 150, 151
Q_V3, Q_R3, Q_NB3, Q_DS2, Q_M3, Q_TA3, Q_TB3, Q_TC3 = 152, 153, 154, 155, 156, 157, 158, 159


def _b6(nc, sems, L):
    g = lambda n: L[n]
    (xres, stt_, g2b, hhalf, zq, thq, dq, uq, tiq, shtile, qkT, ln1T, vpad,
     b1b, b2b, ofc, otile, out_d, ident, ps_acc, ps_a, ps_b, ps_t) = (
        g("xres"), g("stt_"), g("g2b"), g("hhalf"), g("zq"), g("thq"), g("dq"),
        g("uq"), g("tiq"), g("shtile"), g("qkT"), g("ln1T"), g("vpad"),
        g("b1b"), g("b2b"), g("ofc"), g("otile"), g("out_d"), g("ident"),
        g("ps_acc"), g("ps_a"), g("ps_b"), g("ps_t"))
    rstd, ln_merge, n2 = g("rstd"), g("ln_merge"), g("n2")
    MW1, MW2 = W_SC, W_SC + 1

    (s6v, s6a) = sems("s6v", "s6a")
    with nc.Block("b6pre") as blk:
        @blk.vector
        def _(v):
            for i in range(T):
                v.bn_stats(stt_[:, P_BN + i * 12:P_BN + i * 12 + 6],
                           xres[:, i * D:i * D + 512])
                v.bn_stats(stt_[:, P_BN + i * 12 + 6:P_BN + i * 12 + 12],
                           xres[:, i * D + 512:i * D + 1024])
                v.tensor_reduce(stt_[:, P_MX + i:P_MX + i + 1],
                                xres[:, i * D:(i + 1) * D], AX, OP.max)
                v.tensor_reduce(stt_[:, P_MN + i:P_MN + i + 1],
                                xres[:, i * D:(i + 1) * D], AX, OP.min)
            ln_merge(v, T, P_BN, P_M, P_TA, P_TB, P_TC, P_R12)
            n2(v)
            v.scalar_tensor_tensor(stt_[:, P_TB:P_TB + T], stt_[:, P_M:P_M + T],
                                   -1.0, stt_[:, P_MX:P_MX + T], OP.mult, OP.add)
            n2(v)
            v.scalar_tensor_tensor(stt_[:, P_AMAX:P_AMAX + T],
                                   stt_[:, P_MN:P_MN + T], -1.0,
                                   stt_[:, P_M:P_M + T], OP.mult, OP.add)
            n2(v)
            v.tensor_tensor(stt_[:, P_AMAX:P_AMAX + T], stt_[:, P_AMAX:P_AMAX + T],
                            stt_[:, P_TB:P_TB + T], OP.max)
            n2(v)
            v.tensor_scalar(stt_[:, P_TB:P_TB + T], stt_[:, P_TA:P_TA + T], EPS,
                            None, OP.add)
            n2(v)
            v.reciprocal(stt_[:, P_TB:P_TB + T], stt_[:, P_TB:P_TB + T])
            n2(v)
            v.tensor_tensor(stt_[:, P_TB:P_TB + T], stt_[:, P_TA:P_TA + T],
                            stt_[:, P_TB:P_TB + T], OP.mult).then_inc(s6v, 1)
            v.wait_ge(s6a, 1)
            v.tensor_tensor(stt_[:, P_R12:P_R12 + T], stt_[:, P_RS:P_RS + T],
                            stt_[:, P_NB:P_NB + T], OP.mult)
            n2(v)
            v.tensor_tensor(stt_[:, P_TA:P_TA + T], stt_[:, P_AMAX:P_AMAX + T],
                            stt_[:, P_R12:P_R12 + T], OP.mult)
            n2(v)
            v.tensor_scalar(stt_[:, P_TA:P_TA + T], stt_[:, P_TA:P_TA + T],
                            1e-5, None, OP.max)
            n2(v)
            v.reciprocal(stt_[:, P_TB:P_TB + T], stt_[:, P_TA:P_TA + T])
            n2(v)
            v.tensor_tensor(stt_[:, P_TB:P_TB + T], stt_[:, P_TB:P_TB + T],
                            stt_[:, P_R12:P_R12 + T], OP.mult)
            n2(v)
            v.tensor_scalar(stt_[:, P_RS:P_RS + T], stt_[:, P_TB:P_TB + T],
                            127.0, None, OP.mult)
            n2(v)
            v.scalar_tensor_tensor(stt_[:, P_NB:P_NB + T], stt_[:, P_M:P_M + T],
                                   -1.0, stt_[:, P_RS:P_RS + T], OP.mult, OP.mult)
            v.tensor_scalar(stt_[:, P_DS1:P_DS1 + T], stt_[:, P_TA:P_TA + T],
                            -1.0 / 127.0, None, OP.mult)
            n2(v)
            v.tensor_single_scalar(stt_[:, P_DS1:P_DS1 + T],
                                   stt_[:, P_DS1:P_DS1 + T],
                                   stt_[:, MW1:MW1 + 1], OP.mult).then_inc(s6v, 1)

        @blk.scalar
        def _(a):
            a.wait_ge(s6v, 1)
            rstd(a, T, P_TA, P_RS)
            n2(a)
            rstd(a, T, P_TB, P_NB).then_inc(s6a, 1)

    (s7v, s7a, s7p, s7d) = sems("s7v", "s7a", "s7p", "s7d")
    OPS = []
    AC = {"a": 0, "v": 0, "p": 0, "d": 0}

    def em(eng, op, waits=()):
        AC[eng] += 16 if eng == "s" else 1
        OPS.append((eng, op, list(waits), AC[eng]))
        return AC[eng]

    last_tr = 0
    for i in range(_B6T):
        for q in range(5):
            zw = []
            if q == 0:
                if i > 0:
                    zw = [("d", 16 * i), ("v", AC["v"])]
            else:
                zw = [("v", AC["v"])]
            if last_tr:
                zw.append(("p", last_tr))
            a_z = em("a", ("z", i, q), zw)
            v_ti = em("v", ("ti",), [("a", a_z)])
            a_th = em("a", ("th",), [("v", v_ti)])
            v_q = em("v", ("stair",), [("a", a_th)])
            p_tr = em("p", ("tpose", i, q), [("v", v_q)])
            last_tr = p_tr
            a_ev = em("a", ("tev", i, q), [("p", p_tr)])
            if q == 0:
                import os as _os
                NBK = int(_os.environ.get("B6BANKS", "4"))
                for fo in range(8):
                    w = [("a", a_ev)]
                    if fo >= NBK:
                        w.append(("v", AC["v"]))
                        w.append(("a", AC["a"]))
                    p_m = em("p", ("ffn1", i, fo), w)
                    em("v", ("hev", i, fo), [("p", p_m)])
                    if fo in (3, 7):
                        half = fo // 4
                        for kq in range(2):
                            em("a", ("gelu", half, kq), [("v", AC["v"])])
                for kq in range(4):
                    em("a", ("gsq", kq), [("v", AC["v"])])
                em("v", ("gmax",), [("a", AC["a"])])
                em("v", ("g2pre",), [])
                a_r2 = em("a", ("gr2",), [("v", AC["v"])])
                em("v", ("grs2", i), [("a", a_r2)])
        for do in range(2):
            w = [("a", AC["a"])]
            p_m = em("p", ("ffn2", i, do), w)
            em("v", ("oev", i, do), [("p", p_m)])
        v_st = em("v", ("ostat", i), [])
        a_r3 = em("a", ("or3",), [("v", v_st)])
        v_nb = em("v", ("onb",), [("a", a_r3)])
        a_u = em("a", ("ou", i), [("v", v_nb)])
        v_add = em("v", ("oadd", i), [("a", a_u)])
        AC["d"] += 16
        OPS.append(("s", ("ost", i), [("v", v_add)], AC["d"]))

    semmap = {"a": s7a, "v": s7v, "p": s7p, "d": s7d}

    def do_waits(eng_obj, waits):
        for (e, val) in waits:
            eng_obj.wait_ge(semmap[e], val)

    with nc.Block("b6") as blk:
        @blk.scalar
        def _(a):
            for (eng, op, waits, incv) in OPS:
                if eng != "a":
                    continue
                do_waits(a, waits)
                kind = op[0]
                if kind == "z":
                    _, i, q = op
                    if q == 0:
                        a.activation(zq[:], xres[:, i * D:(i + 1) * D], AF.Identity,
                                     scale=stt_[:, P_RS + i:P_RS + i + 1],
                                     bias=stt_[:, P_NB + i:P_NB + i + 1]
                                     ).then_inc(s7a, 1)
                    else:
                        a.activation(zq[:], g2b[:, (q - 1) * 1024:q * 1024], AF.Identity,
                                     scale=stt_[:, Q_RS2:Q_RS2 + 1]
                                     ).then_inc(s7a, 1)
                elif kind == "th":
                    a.activation(thq[:], tiq[:], AF.Identity,
                                 bias=g("negonec")[:]).then_inc(s7a, 1)
                elif kind == "tev":
                    _, i, q = op
                    cp = None
                    for c in range(8):
                        dst = (shtile[:, c * 128:(c + 1) * 128] if (q == 0 or _B6TEVOFF) else
                               g2b[:, (q - 1) * 1024 + c * 128:
                                   (q - 1) * 1024 + (c + 1) * 128])
                        cp = a.copy(dst, ps_t[:, c * 128:(c + 1) * 128])
                    cp.then_inc(s7a, 1)
                elif kind == "gelu":
                    _, half, kq = op
                    a.activation(g2b[:, half * 2048 + kq * 1024:
                                     half * 2048 + (kq + 1) * 1024],
                                 hhalf[:, kq * 1024:(kq + 1) * 1024],
                                 AF.Gelu).then_inc(s7a, 1)
                elif kind == "gsq":
                    kq = op[1]
                    a.activation(hhalf[:, 0:1024],
                                 g2b[:, kq * 1024:(kq + 1) * 1024], AF.Square,
                                 accum_out=stt_[:, Q_SS + kq:Q_SS + kq + 1]
                                 ).then_inc(s7a, 1)
                elif kind == "gr2":
                    a.activation(stt_[:, Q_R2:Q_R2 + 1], stt_[:, Q_MS:Q_MS + 1],
                                 AF.Ln, bias=g("epsc")[:])
                    n2(a)
                    a.activation(stt_[:, Q_R2:Q_R2 + 1], stt_[:, Q_R2:Q_R2 + 1],
                                 AF.Exp, scale=-0.5).then_inc(s7a, 1)
                elif kind == "or3":
                    rstd(a, 1, Q_V3, Q_R3).then_inc(s7a, 1)
                elif kind == "ou":
                    a.activation(otile[:], ofc[:], AF.Identity,
                                 scale=stt_[:, Q_R3:Q_R3 + 1],
                                 bias=stt_[:, Q_NB3:Q_NB3 + 1]).then_inc(s7a, 1)

        @blk.vector
        def _(v):
            for (eng, op, waits, incv) in OPS:
                if eng != "v":
                    continue
                do_waits(v, waits)
                kind = op[0]
                if kind == "ti":
                    v.drain()
                    v.tensor_scalar(tiq[:], zq[:], 1.0, None, OP.mult
                                    ).then_inc(s7v, 1)
                elif kind == "stair":
                    v.tensor_tensor(dq[:], zq[:], thq[:], OP.subtract)
                    n2(v)
                    v.scalar_tensor_tensor(uq[:], dq[:], 0.5, thq[:],
                                           OP.is_ge, OP.add)
                    n2(v)
                    v.scalar_tensor_tensor(zq[:], dq[:], 1.5, uq[:],
                                           OP.is_ge, OP.add).then_inc(s7v, 1)
                elif kind == "hev":
                    import os as _os
                    NBK = int(_os.environ.get("B6BANKS", "4"))
                    _, i, fo = op
                    bank = ps_acc[:, (fo % NBK) * 512:(fo % NBK + 1) * 512]
                    v.scalar_tensor_tensor(
                        hhalf[:, (fo % 4) * 512:(fo % 4 + 1) * 512], bank,
                        stt_[:, P_DS1 + i:P_DS1 + i + 1],
                        b1b[:, fo * 512:(fo + 1) * 512],
                        OP.mult, OP.add).then_inc(s7v, 1)
                elif kind == "gmax":
                    v.drain()
                    v.tensor_reduce(stt_[:, Q_MXA:Q_MXA + 1], g2b[:, 0:2048], AX, OP.max)
                    v.tensor_reduce(stt_[:, Q_MXA + 1:Q_MXA + 2], g2b[:, 2048:4096],
                                    AX, OP.max)
                    v.tensor_reduce(stt_[:, Q_MXA + 2:Q_MXA + 3], g2b[:, 0:2048],
                                    AX, OP.min)
                    v.tensor_reduce(stt_[:, Q_MXA + 3:Q_MXA + 4], g2b[:, 2048:4096],
                                    AX, OP.min).then_inc(s7v, 1)
                elif kind == "g2pre":
                    v.tensor_tensor(stt_[:, Q_MXA:Q_MXA + 1], stt_[:, Q_MXA:Q_MXA + 1],
                                    stt_[:, Q_MXA + 1:Q_MXA + 2], OP.max)
                    v.tensor_tensor(stt_[:, Q_MXA + 2:Q_MXA + 3],
                                    stt_[:, Q_MXA + 2:Q_MXA + 3],
                                    stt_[:, Q_MXA + 3:Q_MXA + 4], OP.min)
                    v.tensor_tensor(stt_[:, Q_SS:Q_SS + 1], stt_[:, Q_SS:Q_SS + 1],
                                    stt_[:, Q_SS + 1:Q_SS + 2], OP.add)
                    v.tensor_tensor(stt_[:, Q_SS + 2:Q_SS + 3],
                                    stt_[:, Q_SS + 2:Q_SS + 3],
                                    stt_[:, Q_SS + 3:Q_SS + 4], OP.add)
                    n2(v)
                    v.scalar_tensor_tensor(stt_[:, Q_MXA:Q_MXA + 1],
                                           stt_[:, Q_MXA + 2:Q_MXA + 3], -1.0,
                                           stt_[:, Q_MXA:Q_MXA + 1], OP.mult, OP.max)
                    v.tensor_tensor(stt_[:, Q_SS:Q_SS + 1], stt_[:, Q_SS:Q_SS + 1],
                                    stt_[:, Q_SS + 2:Q_SS + 3], OP.add)
                    n2(v)
                    v.tensor_scalar(stt_[:, Q_MS:Q_MS + 1], stt_[:, Q_SS:Q_SS + 1],
                                    1.0 / 4096.0, None, OP.mult)
                    v.tensor_scalar(stt_[:, Q_MXA:Q_MXA + 1], stt_[:, Q_MXA:Q_MXA + 1],
                                    1.0, None, OP.mult).then_inc(s7v, 1)
                elif kind == "grs2":
                    v.tensor_tensor(stt_[:, Q_AY2:Q_AY2 + 1], stt_[:, Q_MXA:Q_MXA + 1],
                                    stt_[:, Q_R2:Q_R2 + 1], OP.mult)
                    n2(v)
                    v.tensor_scalar(stt_[:, Q_AY2:Q_AY2 + 1], stt_[:, Q_AY2:Q_AY2 + 1],
                                    1e-5, None, OP.max)
                    n2(v)
                    v.reciprocal(stt_[:, Q_RS2:Q_RS2 + 1], stt_[:, Q_AY2:Q_AY2 + 1])
                    n2(v)
                    v.tensor_tensor(stt_[:, Q_RS2:Q_RS2 + 1], stt_[:, Q_RS2:Q_RS2 + 1],
                                    stt_[:, Q_R2:Q_R2 + 1], OP.mult)
                    n2(v)
                    v.tensor_scalar(stt_[:, Q_RS2:Q_RS2 + 1], stt_[:, Q_RS2:Q_RS2 + 1],
                                    127.0, None, OP.mult)
                    v.tensor_scalar(stt_[:, Q_DS2:Q_DS2 + 1], stt_[:, Q_AY2:Q_AY2 + 1],
                                    -1.0 / 127.0, None, OP.mult)
                    n2(v)
                    v.tensor_single_scalar(stt_[:, Q_DS2:Q_DS2 + 1],
                                           stt_[:, Q_DS2:Q_DS2 + 1],
                                           stt_[:, MW2:MW2 + 1], OP.mult
                                           ).then_inc(s7v, 1)
                elif kind == "oev":
                    _, i, do = op
                    bank = ps_a if do == 0 else ps_b
                    v.scalar_tensor_tensor(ofc[:, do * 512:(do + 1) * 512], bank[:],
                                           stt_[:, Q_DS2:Q_DS2 + 1],
                                           b2b[:, do * 512:(do + 1) * 512],
                                           OP.mult, OP.add).then_inc(s7v, 1)
                elif kind == "ostat":
                    n2(v)
                    v.bn_stats(stt_[:, Q_BN:Q_BN + 6], ofc[:, 0:512])
                    v.bn_stats(stt_[:, Q_BN + 6:Q_BN + 12], ofc[:, 512:1024])
                    ln_merge(v, 1, Q_BN, Q_M3, Q_V3, Q_TA3, Q_TB3, Q_TC3
                             ).then_inc(s7v, 1)
                elif kind == "onb":
                    v.scalar_tensor_tensor(stt_[:, Q_NB3:Q_NB3 + 1],
                                           stt_[:, Q_M3:Q_M3 + 1], -1.0,
                                           stt_[:, Q_R3:Q_R3 + 1], OP.mult, OP.mult
                                           ).then_inc(s7v, 1)
                elif kind == "oadd":
                    i = op[1]
                    v.tensor_tensor(otile[:], otile[:], xres[:, i * D:(i + 1) * D],
                                    OP.add).then_inc(s7v, 1)

        @blk.tensor
        def _(t):
            for (eng, op, waits, incv) in OPS:
                if eng != "p":
                    continue
                do_waits(t, waits)
                kind = op[0]
                if kind == "tpose":
                    mm = None
                    for c in range(8):
                        mm = t.transpose(ps_t[:, c * 128:(c + 1) * 128],
                                         zq[:, c * 128:(c + 1) * 128], ident[:])
                    mm.then_inc(s7p, 1)
                elif kind == "ffn1":
                    import os as _os
                    NBK = int(_os.environ.get("B6BANKS", "4"))
                    _, i, fo = op
                    bank = ps_acc[:, (fo % NBK) * 512:(fo % NBK + 1) * 512]
                    mm = None
                    for dd in range(8):
                        mm = t.matmul(bank, shtile[:, dd * 128:(dd + 1) * 128],
                                      qkT[:, dd * 4096 + fo * 512:
                                          dd * 4096 + (fo + 1) * 512],
                                      start=(dd == 0), stop=(dd == 7))
                    mm.then_inc(s7p, 1)
                elif kind == "ffn2":
                    _, i, do = op
                    bank = ps_a if do == 0 else ps_b
                    mm = None
                    for c in range(32):
                        wsl = (ln1T[:, c * 1024 + do * 512:c * 1024 + (do + 1) * 512]
                               if c < 16 else
                               vpad[:, (c - 16) * 1024 + do * 512:
                                    (c - 16) * 1024 + (do + 1) * 512])
                        mm = t.matmul(bank[:], g2b[:, c * 128:(c + 1) * 128], wsl,
                                      start=(c == 0), stop=(c == 31))
                    mm.then_inc(s7p, 1)

        @blk.sync
        def _(sync):
            for (eng, op, waits, incv) in OPS:
                if eng != "s":
                    continue
                do_waits(sync, waits)
                i = op[1]
                sync.dma_start(out_d[i * 128:(i + 1) * 128, :], otile[:]
                               ).then_inc(s7d, 16)
            sync.wait_ge(s7d, 16 * _B6T)


def _dump(nc, sems, L):
    du = list(_DUMP)
    name, cols, ntiles, coff = du[:4]
    stride = du[4] if len(du) > 4 else cols
    src = L[name] if name in L else L["SB"][name]
    out_d = L["out_d"]
    otile = L["otile"]
    (sdv, sdd) = sems("sdv", "sdd")
    with nc.Block("bdump") as blk:
        @blk.vector
        def _(v):
            for i in range(ntiles):
                if i > 0:
                    v.wait_ge(sdd, 16 * i)
                v.tensor_copy(otile[:, 0:cols],
                              src[:, coff + i * stride:coff + i * stride + cols]
                              ).then_inc(sdv, 1)

        @blk.sync
        def _(sync):
            for i in range(ntiles):
                sync.wait_ge(sdv, i + 1)
                sync.dma_start(out_d[i * 128:(i + 1) * 128, 0:cols],
                               otile[:, 0:cols]).then_inc(sdd, 16)
            sync.wait_ge(sdd, 16 * ntiles)


def _host_inputs(inputs):
    x = np.ascontiguousarray(np.asarray(inputs["L0_attn_hidden_states"]))
    w1t = np.ascontiguousarray(np.asarray(inputs["L0_ffn_w1"]).T)
    w2t = np.ascontiguousarray(np.asarray(inputs["L0_ffn_w2"]).T)
    ident = np.eye(128, dtype=ml_dtypes.bfloat16)
    ii = np.arange(128)
    cmaskv = np.where(ii[:, None] > ii[None, :], np.float32(NEGBIG),
                      np.float32(0.0)).astype(np.float32)
    onesr = np.ones((1, 128), dtype=ml_dtypes.bfloat16)
    base = dict(
        w_attn=np.ascontiguousarray(np.asarray(inputs["L0_attn_w_attn"])),
        b_attn=np.ascontiguousarray(np.asarray(inputs["L0_attn_b_attn"])),
        w_proj=np.ascontiguousarray(np.asarray(inputs["L0_attn_w_proj"])),
        b_proj=np.ascontiguousarray(np.asarray(inputs["L0_attn_b_proj"])),
        w1t=w1t, b1=np.ascontiguousarray(np.asarray(inputs["L0_ffn_b1"])),
        w2t=w2t, b2=np.ascontiguousarray(np.asarray(inputs["L0_ffn_b2"])),
        idm=ident, cmask=cmaskv, onesr=onesr,
    )
    return [dict(base, x=np.ascontiguousarray(x[c])) for c in range(8)]


def kernel(**inputs):
    in_maps = _host_inputs(inputs)
    nc, _ = build_nc()
    res = run_bass_kernel_spmd(nc, in_maps, core_ids=list(range(8)))
    out = np.stack([np.asarray(r["out"]) for r in res.results], axis=0)
    return out.astype(np.float32)
